# version 08
# baseline (speedup 1.0000x reference)
import math
from contextlib import ExitStack

import numpy as np

import concourse.bass as bass
import concourse.mybir as mybir
from concourse.bass_utils import run_bass_kernel_spmd

F32 = mybir.dt.float32
U32 = mybir.dt.uint32
AF = mybir.ActivationFunctionType
AL = mybir.AluOpType
AX = mybir.AxisListType
IOA = bass.IndirectOffsetOnAxis

NEGBIG = -1.0e9
EPS = 1e-5
LN8TH = math.log(0.125)
CONST_VALUES = (0.0, 1.0, 0.125, 0.5, -0.5, -1.0, 16.0, 1e-6, EPS, LN8TH)

ENGINES = ("sync", "tensor", "vector", "scalar", "gpsimd")
OWN_SEM = {"sync": "L", "tensor": "T", "vector": "V", "scalar": "S", "gpsimd": "G"}
ALL_SEMS = ("LB", "LW", "LS", "LU", "LX", "L0", "L1", "LC", "T", "V", "S", "G", "GD")


class Cfg:
    def __init__(self, **kw):
        self.__dict__.update(kw)
        assert self.D % 128 == 0 and self.D == self.NH * 64
        assert self.T % 512 == 0 and self.F % 256 == 0 and self.CAP % 128 == 0
        assert self.GW % 128 == 0 and self.T % self.GW == 0
        assert (self.T // 2) % self.QB == 0 and self.QB <= 512
        assert self.NH % 4 == 0


FULL = Cfg(T=4096, D=1024, NH=16, F=4096, NE=8, CAP=384, GW=256, QB=512)


class Em:

    CHAINED = ("vector", "scalar", "gpsimd")

    def __init__(self):
        self.streams = {e: [] for e in ENGINES}
        self.events = []
        self.latest_dma = {}
        self.last_compute = {e: None for e in ENGINES}
        self.last_by_sem = {}

    def op(self, eng, build, amt=1, sem=None):
        eid = len(self.events)
        if sem is None:
            sem = {"sync": "LC", "gpsimd": "GD" if amt == 16 else "G"}.get(
                eng, OWN_SEM[eng])
        self.events.append([eng, amt, False, sem])
        self.last_by_sem[sem] = eid
        if amt == 16:
            self.latest_dma[sem] = eid
        else:
            if eng in self.CHAINED:
                prev = self.last_compute[eng]
                if prev is not None:
                    self.events[prev][2] = True
                    self.streams[eng].append(("wait", prev, None))
                self.last_compute[eng] = eid
        self.streams[eng].append(("op", build, eid))
        return eid

    def _upgrade(self, eid):
        ev = self.events[eid]
        if ev[1] == 16:
            return self.latest_dma[ev[3]]
        return eid

    def dep(self, eng, *eids):
        for eid in eids:
            if eid is None:
                continue
            eid = self._upgrade(eid)
            self.events[eid][2] = True
            self.streams[eng].append(("wait", eid, None))

    def force_wait(self, eng, *eids):
        for eid in eids:
            if eid is None:
                continue
            eid = self._upgrade(eid)
            self.events[eid][2] = True
            self.streams[eng].append(("wait", eid, None))

    def barrier(self):
        evs = list(self.last_by_sem.values())
        for e in ENGINES:
            self.dep(e, *evs)

    def assign_values(self):
        for ev in self.events:
            if ev[1] == 16:
                ev[2] = True
        cnt = {s: 0 for s in ALL_SEMS}
        self.val = {}
        for eng in ENGINES:
            for kind, a, b in self.streams[eng]:
                if kind == "op" and self.events[b][2]:
                    s = self.events[b][3]
                    cnt[s] += self.events[b][1]
                    self.val[b] = cnt[s]

    def emit_engine(self, eng, h, sems):
        waited = {s: 0 for s in ALL_SEMS}
        for kind, a, b in self.streams[eng]:
            if kind == "op":
                inst = a(h)
                if self.events[b][2]:
                    inst.then_inc(sems[self.events[b][3]], self.events[b][1])
            else:
                s = self.events[a][3]
                v = self.val[a]
                if waited[s] < v:
                    waited[s] = v
                    h.wait_ge(sems[s], v)


class Arena:
    def __init__(self, words):
        self.words = words
        self.live = {}
        self.frozen = {}
        self.top = 0
        self.peak = 0
        self.handle = None

    def alloc(self, name, words):
        words = (words + 7) & ~7
        assert name not in self.frozen, f"arena name reused: {name}"
        self.live[name] = (self.top, words)
        self.frozen[name] = (self.top, words)
        self.top += words
        self.peak = max(self.peak, self.top)
        if self.top > self.words:
            raise MemoryError(f"arena overflow at {name}: {self.top} > {self.words}")

    def free(self, *names):
        for n in names:
            self.live.pop(n)
        self.top = max([o + w for o, w in self.live.values()], default=0)

    def ap(self, name, p0=0, p1=128, c0=0, c1=None):
        off, w = self.frozen[name]
        if c1 is None:
            c1 = w
        assert 0 <= c0 <= c1 <= w, (name, c0, c1, w)
        return self.handle[p0:p1, off + c0:off + c1]


def build_program(C, debug=False):
    nc = bass.Bass("TRN2", target_bir_lowering=False)
    T, D, NH, F, NE, CAP = C.T, C.D, C.NH, C.F, C.NE, C.CAP
    GW, QB = C.GW, C.QB
    DC = D // 128
    TT = T // 128
    OWN = T // 2
    OT = OWN // 128
    HPR = NH // 2
    HT = HPR // 2
    HW = HPR * 64
    SC = (NE * CAP) // 128
    FC = F // 128
    GT = GW // 128
    NQB = OWN // QB
    KC = TT
    CS = CAP // 128
    DH = max(1, D // 512)
    DW = min(512, D)

    scratch_kind = "ExternalOutput" if debug else "Internal"

    def din(name, shape, dtype=F32):
        return nc.dram_tensor(name, shape, dtype, kind="ExternalInput").ap()

    x0 = din("x0", [T, D])
    x0T = din("x0T", [D, T])
    u_in = din("u", [T, T])
    wq1T = din("wq1T", [D, D])
    wk1T = din("wk1T", [D, D])
    bq1 = din("bq1", [D])
    bk1 = din("bk1", [D])
    wq2 = din("wq2", [D, D])
    wk2 = din("wk2", [D, D])
    wv2 = din("wv2", [D, D])
    wo2 = din("wo2", [D, D])
    bq2 = din("bq2", [D])
    bk2 = din("bk2", [D])
    bv2b = din("bv2b", [128, D])
    bo2b = din("bo2b", [128, D])
    g1b = din("g1b", [128, D])
    b1lnb = din("b1lnb", [128, D])
    g2b = din("g2b", [128, D])
    b2lnb = din("b2lnb", [128, D])
    gwD = din("gw", [D, NE])
    gbb = din("gbb", [128, NE])
    w1 = din("w1", [NE, D, F])
    b1m = din("b1m", [NE, F])
    w2 = din("w2", [NE, F, D])
    b2m = din("b2m", [NE, D])
    cpool_in = din("cpool", [128, 16])
    ident_in = din("ident", [128, 128])
    trimask_in = din("trimask", [128, 128])
    ones_in = din("ones", [128, 128])
    e2_in = din("e2", [128, 2])
    basecol_in = din("basecol", [NE, 1])
    ownidx_in = din("ownidx", [OWN, 1], U32)

    y_out = nc.dram_tensor("y", [OWN, D], F32, kind="ExternalOutput").ap()

    x1_hbm = nc.dram_tensor("x1_hbm", [T, D], F32, kind=scratch_kind).ap()
    x2_hbm = nc.dram_tensor("x2_hbm", [OWN, D], F32, kind=scratch_kind).ap()
    hn2_hbm = nc.dram_tensor("hn2_hbm", [OWN, D], F32, kind=scratch_kind).ap()
    attoT_hbm = nc.dram_tensor("attoT_hbm", [D, OWN], F32, kind=scratch_kind).ap()
    Y_hbm = nc.dram_tensor("Y_hbm", [NE * CAP, D], F32, kind=scratch_kind).ap()
    invtok_hbm = nc.dram_tensor("invtok_hbm", [NE * CAP, 1], U32, kind=scratch_kind).ap()
    if debug:
        dbg_dest = nc.dram_tensor("dbg_dest", [1, OWN], F32, kind="ExternalOutput").ap()

    E = Em()
    A = Arena(words=52900)
    AU = Arena(words=3 * ((OT + 7) & ~7) + 2 * ((SC + 7) & ~7) + 16)
    ctx = ExitStack()
    A.handle = ctx.enter_context(nc.sbuf_tensor("arena", [128, A.words], F32))
    AU.handle = ctx.enter_context(nc.sbuf_tensor("uarena", [128, AU.words], U32))
    psum = [ctx.enter_context(nc.psum_tensor(f"pb{i}", [128, 512], F32)) for i in range(8)]
    sems = {s: ctx.enter_context(nc.semaphore(f"sem{s}")) for s in ALL_SEMS}

    def mm(out_ap, lhsT_ap, rhs_ap, start=True, stop=True):
        return E.op("tensor", lambda e: e.matmul(out_ap, lhsT_ap, rhs_ap, start=start, stop=stop))

    def ld(out_ap, in_ap, slow=False, stream="LS"):
        return E.op("sync", lambda e: e.dma_start(out=out_ap, in_=in_ap,
                                                  allow_slow_non_contiguous=slow),
                    amt=16, sem=stream)

    def st(out_ap, in_ap, slow=False):
        return E.op("gpsimd", lambda e: e.dma_start(out=out_ap, in_=in_ap,
                                                    allow_slow_non_contiguous=slow), amt=16)

    vop = lambda b: E.op("vector", b)
    sop = lambda b: E.op("scalar", b)
    gop = lambda b, amt=1: E.op("gpsimd", b, amt=amt)

    BB = {}

    def bdep(*bis):
        E.dep("tensor", *[BB.get(b) for b in bis])

    AU.alloc("ownidx", (OT + 7) & ~7)

    def ld_x1_own(out_ap, i):
        return E.op("gpsimd", lambda e: e.indirect_dma_start(
            out=out_ap, out_offset=None, in_=x1_hbm[:, :],
            in_offset=IOA(AU.ap("ownidx", 0, 128, i, i + 1), 0)), amt=16)

    for nm, w in (("ident", 128), ("trimask", 128), ("ones", 128), ("e2", 8),
                  ("basecol", 8), ("stat", 8), ("small", 32), ("bq1c", DC),
                  ("bk1c", DC)):
        A.alloc(nm, w)
    A.alloc("cpool", 16)
    ld(A.ap("cpool"), cpool_in[:], stream="LB")
    for ci, cv in enumerate(CONST_VALUES):
        nc.const_aps.register(F32, cv, A.ap("cpool", 0, 128, ci, ci + 1))
    ld(A.ap("ident"), ident_in[:], stream="LB")
    ld(A.ap("trimask"), trimask_in[:], stream="LB")
    ld(A.ap("ones"), ones_in[:], stream="LB")
    ld(A.ap("e2", 0, 128, 0, 2), e2_in[:], stream="LB")
    ld(A.ap("basecol", 0, NE, 0, 1), basecol_in[:], stream="LB")
    ld(A.ap("bq1c", 0, 128, 0, DC), bq1.rearrange("(c p) -> p c", p=128), slow=True, stream="LB")
    ld(A.ap("bk1c", 0, 128, 0, DC), bk1.rearrange("(c p) -> p c", p=128), slow=True, stream="LB")
    e_consts = ld(AU.ap("ownidx", 0, 128, 0, OT),
                  ownidx_in.rearrange("(c p) o -> p (c o)", p=128), slow=True, stream="LB")
    IDN = A.ap("ident")
    TRI = A.ap("trimask")
    ONES = A.ap("ones")

    def tr(out_ap, in_ap):
        p = in_ap.shape[0]
        return E.op("tensor", lambda e: e.transpose(out_ap, in_ap, A.ap("ident", 0, p, 0, p)))

    for _eng in ("tensor", "vector", "scalar", "gpsimd"):
        E.dep(_eng, e_consts)
    E.barrier()

    def layer_norm(t_ap, out_ap, junk_ap, g_ap=None, b_ap=None, deps=()):
        s_sum = A.ap("stat", 0, 128, 0, 1)
        s_ssq = A.ap("stat", 0, 128, 1, 2)
        s_mean = A.ap("stat", 0, 128, 2, 3)
        s_var = A.ap("stat", 0, 128, 3, 4)
        s_rstd = A.ap("stat", 0, 128, 4, 5)
        E.dep("vector", *deps)
        vop(lambda e: e.reduce_sum(s_sum, t_ap, axis=AX.X))
        E.dep("scalar", *deps)
        ev_sq = sop(lambda e: e.activation(junk_ap, t_ap, AF.Square, accum_out=s_ssq))
        E.dep("vector", ev_sq)
        vop(lambda e: e.tensor_scalar(s_mean, s_sum, 1.0 / D, None, AL.mult))
        vop(lambda e: e.tensor_tensor(s_var, s_mean, s_mean, AL.mult))
        ev5 = vop(lambda e: e.scalar_tensor_tensor(s_var, s_ssq, 1.0 / D, s_var, AL.mult, AL.subtract))
        E.dep("scalar", ev5)
        sop(lambda e: e.activation(s_rstd, s_var, AF.Ln, bias=EPS))
        ev7 = sop(lambda e: e.activation(s_rstd, s_rstd, AF.Exp, scale=-0.5))
        E.dep("vector", ev7)
        ev8 = vop(lambda e: e.tensor_scalar(out_ap, t_ap, s_mean, s_rstd, AL.subtract, AL.mult))
        if g_ap is not None:
            vop(lambda e: e.tensor_tensor(out_ap, out_ap, g_ap, AL.mult))
            ev8 = vop(lambda e: e.tensor_tensor(out_ap, out_ap, b_ap, AL.add))
        return ev8, ev_sq

    A.alloc("KT", DC * T)

    def KTap(fc, c0, c1, p0=0, p1=128):
        return A.ap("KT", p0, p1, fc * T + c0, fc * T + c1)

    nPB = T // 512
    A.alloc("lnmkb", nPB * NH)
    A.alloc("wfull", DC * D)
    A.alloc("x0Tg", DC * 512)

    def wfap(fi, c0, c1):
        return A.ap("wfull", 0, 128, fi * D + c0, fi * D + c1)

    ev_wk = [ld(wfap(fi, 0, D), wk1T[fi * 128:(fi + 1) * 128, :], stream="LB") for fi in range(DC)]

    TB = 512
    kt_done = {}
    xg_cons = []
    for tb in range(T // TB):
        E.dep("sync", *xg_cons)
        xg_cons = []
        evs_x = [ld(A.ap("x0Tg", 0, 128, fi * TB, fi * TB + TB),
                    x0T[fi * 128:(fi + 1) * 128, tb * TB:(tb + 1) * TB], stream="LB") for fi in range(DC)]
        for fo in range(DC):
            bank = fo % 2
            E.dep("tensor", *evs_x, *ev_wk)
            bdep(bank)
            e_mm = None
            for fi in range(DC):
                e_mm = mm(psum[bank][:, 0:TB], wfap(fi, fo * 128, fo * 128 + 128),
                          A.ap("x0Tg", 0, 128, fi * TB, fi * TB + TB),
                          start=(fi == 0), stop=(fi == DC - 1))
            xg_cons.append(e_mm)
            E.dep("scalar", e_mm)
            e_ev = sop(lambda e, b=bank, fo=fo, tb=tb: e.activation(
                KTap(fo, tb * TB, tb * TB + TB), psum[b][:, 0:TB], AF.Identity,
                bias=A.ap("bk1c", 0, 128, fo, fo + 1)))
            BB[bank] = e_ev
            kt_done[fo] = e_ev

    A.alloc("sqb", 512)
    A.alloc("mkrows", nPB * NH)
    mk_done = []
    sqb_cons = None
    for fc in range(DC):
        run_ev = None
        for cb in range(nPB):
            E.dep("scalar", kt_done[fc], sqb_cons)
            e_sq = sop(lambda e, fc=fc, cb=cb: e.activation(
                A.ap("sqb"), KTap(fc, cb * 512, cb * 512 + 512), AF.Square))
            E.dep("tensor", e_sq)
            bdep(2)
            e_n = mm(psum[2][0:2, 0:512], A.ap("e2", 0, 128, 0, 2), A.ap("sqb"))
            sqb_cons = e_n
            E.dep("vector", e_n)
            e_r = vop(lambda e: e.reduce_max(A.ap("small", 0, 2, 0, 1), psum[2][0:2, 0:512], axis=AX.X))
            BB[2] = e_r
            if cb == 0:
                run_ev = vop(lambda e: e.tensor_copy(A.ap("small", 0, 2, 2, 3), A.ap("small", 0, 2, 0, 1)))
            else:
                run_ev = vop(lambda e: e.tensor_tensor(A.ap("small", 0, 2, 2, 3),
                                                       A.ap("small", 0, 2, 2, 3),
                                                       A.ap("small", 0, 2, 0, 1), AL.max))
            E.dep("tensor", run_ev)
            bdep(3)
            e_t = tr(psum[3][0:1, 0:2], A.ap("small", 0, 2, 2, 3))
            E.dep("vector", e_t)
            e_cpy = vop(lambda e, fc=fc, cb=cb: e.tensor_copy(
                A.ap("mkrows", 0, 1, cb * NH + 2 * fc, cb * NH + 2 * fc + 2), psum[3][0:1, 0:2]))
            BB[3] = e_cpy
            mk_done.append(e_cpy)
    E.dep("scalar", *mk_done)
    e_lnmk = sop(lambda e: e.activation(A.ap("mkrows", 0, 1, 0, nPB * NH),
                                        A.ap("mkrows", 0, 1, 0, nPB * NH), AF.Ln))
    for pb in range(nPB):
        E.dep("tensor", e_lnmk)
        bdep(3)
        e_bc = mm(psum[3][:, 0:NH], ONES[0:1, :], A.ap("mkrows", 0, 1, pb * NH, pb * NH + NH))
        E.dep("vector", e_bc)
        BB[3] = vop(lambda e, pb=pb: e.tensor_copy(A.ap("lnmkb", 0, 128, pb * NH, pb * NH + NH),
                                                   psum[3][:, 0:NH]))
    A.free("sqb", "mkrows", "x0Tg", "wfull")
    E.barrier()
    BB.clear()

    HSTR = D + 8
    A.alloc("avg", T)
    A.alloc("PA", T)
    A.alloc("PB", max(T, 3 * HSTR))
    A.alloc("QTg", DC * GW)
    A.alloc("x0Tq", DC * GW)
    WSW = min(D, 4 * 128)
    A.alloc("wslab", WSW)
    A.alloc("sch", 16)
    A.alloc("negc", NH)
    A.alloc("qm2", NH)
    A.alloc("ush", 512)
    A.alloc("tsh", 512)
    A.alloc("dwT", 256)
    A.alloc("x1t", D)

    def QTg_ap(fc, c0, c1, p0=0, p1=128):
        return A.ap("QTg", p0, p1, fc * GW + c0, fc * GW + c1)

    def hid_ap(par, c0, c1):
        return A.ap("PB", 0, 128, par * HSTR + c0, par * HSTR + c1)

    def x0t_ap(c0=0, c1=None):
        return A.ap("PB", 0, 128, 2 * HSTR + c0, 2 * HSTR + (c1 if c1 is not None else D))

    qtg_cons = []
    x1_store_last = None
    x1t_busy = None
    pa_scal_dep = None
    pb_scal_dep = []
    avg_scal_dep = None
    ush_cons = []
    prev_p2 = None
    nwave = max(1, DC // 4)
    per_wave = DC // nwave

    for g in range(T // GW):
        E.dep("sync", *qtg_cons)
        qtg_cons = []
        evs_x = [ld(A.ap("x0Tq", 0, 128, fi * GW, fi * GW + GW),
                    x0T[fi * 128:(fi + 1) * 128, g * GW:g * GW + GW], stream="LB") for fi in range(DC)]
        qt_ev = {}
        slab_cons = None
        for wave in range(nwave):
            fos = list(range(wave * per_wave, (wave + 1) * per_wave))
            fo0 = fos[0]
            for sl in range(DC):
                E.dep("sync", slab_cons)
                e_w = ld(A.ap("wslab"),
                         wq1T[sl * 128:(sl + 1) * 128, fo0 * 128:fo0 * 128 + WSW])
                for i, fo in enumerate(fos):
                    bi = 4 + (i % 4)
                    E.dep("tensor", e_w, evs_x[sl])
                    if sl == 0:
                        bdep(bi)
                    e_mm = mm(psum[bi][:, 0:GW],
                              A.ap("wslab", 0, 128, (fo - fo0) * 128, (fo - fo0) * 128 + 128),
                              A.ap("x0Tq", 0, 128, sl * GW, sl * GW + GW),
                              start=(sl == 0), stop=(sl == DC - 1))
                    if sl == DC - 1:
                        E.dep("scalar", e_mm)
                        qt_ev[fo] = sop(lambda e, bi=bi, fo=fo: e.activation(
                            QTg_ap(fo, 0, GW), psum[bi][:, 0:GW], AF.Identity,
                            bias=A.ap("bq1c", 0, 128, fo, fo + 1)))
                        BB[bi] = qt_ev[fo]
                slab_cons = e_mm
        for ti in range(GT):
            gi = g * GT + ti
            ext = (gi + 1) * 128
            chunks = []
            off = 0
            while off < ext:
                w = min(512, ext - off)
                chunks.append((off, w))
                off += w
            dcol = ext - 128
            pb = (ext - 1) // 512

            E.dep("sync", prev_p2, avg_scal_dep)
            uld_evs = [ld(A.ap("avg", 0, 128, off, off + w),
                          u_in[gi * 128:(gi + 1) * 128, off:off + w], stream="LU")
                       for (off, w) in chunks]

            qn_done = []
            tsh_cons2 = None
            for fc in range(DC):
                E.dep("scalar", qt_ev[fc], *ush_cons)
                ush_cons = []
                e_sq = sop(lambda e, fc=fc, ti=ti: e.activation(
                    A.ap("ush", 0, 128, 0, 128), QTg_ap(fc, ti * 128, ti * 128 + 128), AF.Square))
                E.dep("tensor", e_sq)
                bdep(2)
                e_n = mm(psum[2][0:2, 0:128], A.ap("e2", 0, 128, 0, 2), A.ap("ush", 0, 128, 0, 128))
                ush_cons.append(e_n)
                E.dep("vector", e_n, tsh_cons2)
                e_cp = vop(lambda e: e.tensor_copy(A.ap("tsh", 0, 2, 0, 128), psum[2][0:2, 0:128]))
                BB[2] = e_cp
                E.dep("tensor", e_cp)
                bdep(3)
                e_t = tr(psum[3][0:128, 0:2], A.ap("tsh", 0, 2, 0, 128))
                tsh_cons2 = e_t
                E.dep("vector", e_t)
                e_q = vop(lambda e, fc=fc: e.tensor_copy(A.ap("qm2", 0, 128, 2 * fc, 2 * fc + 2),
                                                         psum[3][0:128, 0:2]))
                BB[3] = e_q
                qn_done.append(e_q)
            E.dep("scalar", *qn_done)
            e_lnq = sop(lambda e: e.activation(A.ap("qm2", 0, 128, 0, NH),
                                               A.ap("qm2", 0, 128, 0, NH), AF.Ln))
            E.dep("vector", e_lnq)
            e_vadd = vop(lambda e, pb=pb: e.tensor_tensor(A.ap("qm2", 0, 128, 0, NH),
                                                          A.ap("qm2", 0, 128, 0, NH),
                                                          A.ap("lnmkb", 0, 128, pb * NH, pb * NH + NH),
                                                          AL.add))
            E.dep("scalar", e_vadd)
            e_exp = sop(lambda e: e.activation(A.ap("qm2", 0, 128, 0, NH), A.ap("qm2", 0, 128, 0, NH),
                                               AF.Exp, scale=0.5, bias=LN8TH))
            E.dep("vector", e_exp)
            e_negc = vop(lambda e: e.tensor_scalar(A.ap("negc", 0, 128, 0, NH),
                                                   A.ap("qm2", 0, 128, 0, NH), -1.0, None, AL.mult))

            accA = accB = None
            first_pa_dep = [pa_scal_dep, avg_scal_dep]
            first_pb_dep = list(pb_scal_dep)
            pb_scal_dep = []
            for p in range(NH // 2):
                hA, hB = 2 * p, 2 * p + 1
                exp_last = {}
                for ci, (off, w) in enumerate(chunks):
                    for (hh, bi, prow) in ((hA, 0 + (ci % 2), 0), (hB, 2 + (ci % 2), 64)):
                        E.dep("tensor", qt_ev[p], e_negc)
                        bdep(bi)
                        e_mm = mm(psum[bi][:, 0:w],
                                  QTg_ap(p, ti * 128, ti * 128 + 128, prow, prow + 64),
                                  KTap(p, off, off + w, prow, prow + 64))
                        qtg_cons.append(e_mm)
                        mask_ev = None
                        if off + w == ext:
                            E.dep("vector", e_mm)
                            mask_ev = vop(lambda e, bi=bi, o=dcol - off: e.tensor_tensor(
                                psum[bi][:, o:o + 128], psum[bi][:, o:o + 128], TRI, AL.add))
                        Pbuf = "PA" if hh == hA else "PB"
                        scol = 0 if hh == hA else 8
                        E.dep("scalar", e_mm, mask_ev)
                        if p == 0 and ci == 0:
                            E.dep("scalar", *(first_pa_dep if Pbuf == "PA" else first_pb_dep))
                        E.dep("scalar", accA if hh == hA else accB)
                        e_x = sop(lambda e, bi=bi, off=off, w=w, hh=hh, Pbuf=Pbuf, scol=scol, ci=ci:
                                  e.activation(A.ap(Pbuf, 0, 128, off, off + w), psum[bi][:, 0:w],
                                               AF.Exp, scale=0.125,
                                               bias=A.ap("negc", 0, 128, hh, hh + 1),
                                               accum_out=A.ap("sch", 0, 128, scol + ci, scol + ci + 1)))
                        BB[bi] = e_x
                        exp_last[hh] = e_x
                e_s16 = None
                for (col, hh, scol) in ((10, hA, 0), (11, hB, 8)):
                    E.dep("vector", exp_last[hh])
                    vop(lambda e, col=col, scol=scol, n=len(chunks): e.reduce_sum(
                        A.ap("small", 0, 128, col, col + 1), A.ap("sch", 0, 128, scol, scol + n),
                        axis=AX.X))
                    e_s16 = vop(lambda e, col=col: e.tensor_scalar(
                        A.ap("small", 0, 128, col, col + 1), A.ap("small", 0, 128, col, col + 1),
                        16.0, 1e-37, AL.mult, AL.max))
                E.dep("scalar", e_s16)
                sop(lambda e: e.activation(A.ap("small", 0, 128, 12, 14),
                                           A.ap("small", 0, 128, 10, 12), AF.Ln))
                e_rexp = sop(lambda e: e.activation(A.ap("small", 0, 128, 12, 14),
                                                    A.ap("small", 0, 128, 12, 14), AF.Exp, scale=-1.0))
                for (col, hh, Pbuf) in ((12, hA, "PA"), (13, hB, "PB")):
                    if p == 0 and hh == hA:
                        E.dep("vector", e_rexp, *uld_evs)
                    else:
                        E.dep("vector", e_rexp)
                    e_acc = vop(lambda e, Pbuf=Pbuf, ext=ext, col=col: e.scalar_tensor_tensor(
                        A.ap("avg", 0, 128, 0, ext), A.ap(Pbuf, 0, 128, 0, ext),
                        A.ap("small", 0, 128, col, col + 1), A.ap("avg", 0, 128, 0, ext),
                        AL.mult, AL.add))
                    if hh == hA:
                        accA = e_acc
                    else:
                        accB = e_acc

            E.dep("vector", accA, accB)
            e_msk = vop(lambda e, dcol=dcol: e.tensor_tensor(
                A.ap("avg", 0, 128, dcol, dcol + 128),
                A.ap("avg", 0, 128, dcol, dcol + 128), TRI, AL.add))
            E.dep("scalar", e_msk, pa_scal_dep)
            e_p2 = sop(lambda e, ext=ext: e.activation(A.ap("PA", 0, 128, 0, ext),
                                                       A.ap("avg", 0, 128, 0, ext), AF.Exp, scale=0.5))
            prev_p2 = e_p2

            hid_busy = [None, None]
            e_mm3 = None
            for kc in range(gi + 1):
                par = kc % 2
                E.dep("tensor", e_p2)
                bdep(4 + par)
                e_tr = tr(psum[4 + par][:, 0:128], A.ap("PA", 0, 128, kc * 128, kc * 128 + 128))
                pa_scal_dep = e_tr
                E.dep("vector", e_tr)
                e_dw = vop(lambda e, par=par: e.tensor_copy(
                    A.ap("dwT", 0, 128, par * 128, par * 128 + 128), psum[4 + par][:, 0:128]))
                BB[4 + par] = e_dw
                if kc < 2:
                    E.dep("sync", accB)
                E.dep("sync", hid_busy[par])
                e_h = ld(hid_ap(par, 0, D), x0[kc * 128:(kc + 1) * 128, :],
                         stream=f"L{par}")
                E.dep("vector", e_h)
                e_o1 = vop(lambda e, par=par: e.memset(hid_ap(par, D, D + 1), 1.0))
                E.dep("tensor", e_dw, e_o1)
                if kc == 0:
                    E.dep("tensor", x1t_busy)
                    bdep(*range(DH + 1))
                for s in range(DH):
                    mm(psum[s][:, 0:DW], A.ap("dwT", 0, 128, par * 128, par * 128 + 128),
                       hid_ap(par, s * DW, s * DW + DW), start=(kc == 0), stop=(kc == gi))
                e_mm3 = mm(psum[DH][0:128, 0:1], A.ap("dwT", 0, 128, par * 128, par * 128 + 128),
                           hid_ap(par, D, D + 1), start=(kc == 0), stop=(kc == gi))
                hid_busy[par] = e_mm3
                pb_scal_dep.append(e_mm3)

            E.dep("sync", accB, *first_pb_dep)
            e_x0 = ld(x0t_ap(), x0[gi * 128:(gi + 1) * 128, :], stream="LX")
            E.dep("vector", e_mm3)
            e_cl = vop(lambda e: e.tensor_scalar(A.ap("small", 0, 128, 8, 9), psum[DH][0:128, 0:1],
                                                 1e-6, None, AL.max))
            BB[DH] = e_cl
            vop(lambda e: e.reciprocal(A.ap("small", 0, 128, 9, 10), A.ap("small", 0, 128, 8, 9)))
            E.dep("vector", e_x0, x1t_busy)
            ep_evs = []
            for s in range(DH):
                e_p = vop(lambda e, s=s: e.scalar_tensor_tensor(
                    A.ap("x1t", 0, 128, s * DW, s * DW + DW), psum[s][:, 0:DW],
                    A.ap("small", 0, 128, 9, 10), x0t_ap(s * DW, s * DW + DW),
                    AL.mult, AL.add))
                BB[s] = e_p
                ep_evs.append(e_p)
            pb_scal_dep.extend(ep_evs)
            e_ln, e_lnsq = layer_norm(A.ap("x1t"), A.ap("x1t"), A.ap("avg", 0, 128, 0, D), deps=ep_evs)
            avg_scal_dep = e_lnsq
            E.dep("gpsimd", e_ln, e_lnsq)
            x1t_busy = st(x1_hbm[gi * 128:(gi + 1) * 128, :], A.ap("x1t"))
            x1_store_last = x1t_busy

    A.free("avg", "PA", "PB", "QTg", "x0Tq", "wslab", "sch", "negc", "qm2",
           "ush", "tsh", "dwT", "x1t", "KT", "lnmkb")
    E.barrier()
    BB.clear()

    QSEG = OWN // 2
    NSEG = 2
    NQBS = QSEG // QB
    assert NQBS >= 1 and QSEG % QB == 0
    A.alloc("KT2", HT * T)
    A.alloc("V2", TT * (HPR * 65))
    A.alloc("QT2", HT * QSEG)
    A.alloc("wbuf", DC * 512)
    A.alloc("hnT", DC * 128)
    A.alloc("hnt", D)
    A.alloc("x1b", D)
    A.alloc("PT", 4 * 512)
    A.alloc("att", 512)
    A.alloc("Rr", 512)
    A.alloc("Rb", 512)
    A.alloc("b2c", 2 * DC)
    A.alloc("qmk2", 40)
    A.alloc("negc2", HPR)

    def KT2ap(j, c0, c1, p0=0, p1=128):
        return A.ap("KT2", p0, p1, j * T + c0, j * T + c1)

    def V2ap(kt, h, c0, c1):
        base = kt * (HPR * 65) + h * 65
        return A.ap("V2", 0, 128, base + c0, base + c1)

    def V2full(kt):
        return A.ap("V2", 0, 128, kt * (HPR * 65), (kt + 1) * (HPR * 65))

    def QT2ap(j, c0, c1, p0=0, p1=128):
        return A.ap("QT2", p0, p1, j * QSEG + c0, j * QSEG + c1)

    A.alloc("lnb1", 2 * D)
    A.alloc("bvbs", D)
    ld(A.ap("b2c", 0, 128, 0, DC), bq2.rearrange("(c p) -> p c", p=128), slow=True, stream="LB")
    ld(A.ap("b2c", 0, 128, DC, 2 * DC), bk2.rearrange("(c p) -> p c", p=128), slow=True, stream="LB")
    ld(A.ap("lnb1", 0, 128, 0, D), g1b[:, :], stream="LB")
    ld(A.ap("lnb1", 0, 128, D, 2 * D), b1lnb[:, :], stream="LB")
    e_ph3c = ld(A.ap("bvbs", 0, 128, 0, D), bv2b[:, :], stream="LB")
    for _eng in ("tensor", "vector", "scalar", "gpsimd"):
        E.dep(_eng, e_ph3c)
    G1AP = A.ap("lnb1", 0, 128, 0, D)
    B1AP = A.ap("lnb1", 0, 128, D, 2 * D)

    atto_store_last = None
    att_busy = None
    for r in range(2):
        hbase = r * HPR
        fbase = r * HW

        wb_cons = []
        hn_last = (None, None)
        hnT_cons = None
        for which in ("K", "V"):
            E.dep("sync", *wb_cons)
            wb_cons = []
            wsrc = {"K": wk2, "V": wv2}[which]
            ev_wb = [ld(A.ap("wbuf", 0, 128, fi * 512, fi * 512 + HW),
                        wsrc[fi * 128:(fi + 1) * 128, fbase:fbase + HW], stream="LW")
                     for fi in range(DC)]
            for kt in range(TT):
                E.dep("sync", hn_last[0], hn_last[1], x1_store_last)
                e_x1 = ld(A.ap("x1b"), x1_hbm[kt * 128:(kt + 1) * 128, :])
                junk = A.ap("QT2", 0, 128, 0, D)
                e_hn, e_hnsq = layer_norm(A.ap("x1b"), A.ap("hnt"), junk,
                                          g_ap=G1AP, b_ap=B1AP, deps=[e_x1])
                hn_last = (e_hn, e_hnsq)
                trev = {}
                for fc in range(DC):
                    E.dep("tensor", e_hn)
                    bdep(4 + fc % 2)
                    e_tp = tr(psum[4 + fc % 2][:, 0:128],
                              A.ap("hnt", 0, 128, fc * 128, fc * 128 + 128))
                    E.dep("vector", e_tp, hnT_cons)
                    e_hT = vop(lambda e, fc=fc: e.tensor_copy(
                        A.ap("hnT", 0, 128, fc * 128, fc * 128 + 128), psum[4 + fc % 2][:, 0:128]))
                    BB[4 + fc % 2] = e_hT
                    trev[fc] = e_hT
                if which == "K":
                    for j in range(HT):
                        bi = j % 2
                        E.dep("tensor", *ev_wb)
                        bdep(bi)
                        e_mmx = None
                        for fi in range(DC):
                            E.dep("tensor", trev[fi])
                            e_mmx = mm(psum[bi][:, 0:128],
                                       A.ap("wbuf", 0, 128, fi * 512 + j * 128, fi * 512 + j * 128 + 128),
                                       A.ap("hnT", 0, 128, fi * 128, fi * 128 + 128),
                                       start=(fi == 0), stop=(fi == DC - 1))
                        wb_cons.append(e_mmx)
                        hnT_cons = e_mmx
                        E.dep("scalar", e_mmx)
                        e_evx = sop(lambda e, bi=bi, j=j, kt=kt, fbase=fbase: e.activation(
                            KT2ap(j, kt * 128, kt * 128 + 128), psum[bi][:, 0:128], AF.Identity,
                            bias=A.ap("b2c", 0, 128, DC + (fbase // 128) + j,
                                      DC + (fbase // 128) + j + 1)))
                        BB[bi] = e_evx
                else:
                    E.dep("tensor", *ev_wb)
                    bdep(2)
                    e_mmv = None
                    for fi in range(DC):
                        E.dep("tensor", trev[fi])
                        e_mmv = mm(psum[2][:, 0:HW],
                                   A.ap("hnT", 0, 128, fi * 128, fi * 128 + 128),
                                   A.ap("wbuf", 0, 128, fi * 512, fi * 512 + HW),
                                   start=(fi == 0), stop=(fi == DC - 1))
                    wb_cons.append(e_mmv)
                    hnT_cons = e_mmv
                    E.dep("vector", e_mmv)
                    e_vv = vop(lambda e, kt=kt, fbase=fbase: e.tensor_tensor(
                        V2full(kt).rearrange("p (h x) -> p h x", x=65)[:, :, 0:64],
                        psum[2][:, 0:HW].rearrange("p (h x) -> p h x", x=64),
                        A.ap("bvbs", 0, 128, fbase, fbase + HW).rearrange("p (h x) -> p h x", x=64),
                        AL.add))
                    BB[2] = e_vv
                    vop(lambda e, kt=kt: e.memset(
                        V2full(kt).rearrange("p (h x) -> p h x", x=65)[:, :, 64:65], 1.0))

        E.barrier()
        BB.clear()
        qm_done = []
        sq_cons = None
        for j in range(HT):
            run_ev = None
            for cb in range(T // 512):
                E.dep("scalar", sq_cons)
                e_sq = sop(lambda e, j=j, cb=cb: e.activation(
                    A.ap("PT", 0, 128, 0, 512), KT2ap(j, cb * 512, cb * 512 + 512), AF.Square))
                E.dep("tensor", e_sq)
                bdep(2)
                e_n = mm(psum[2][0:2, 0:512], A.ap("e2", 0, 128, 0, 2), A.ap("PT", 0, 128, 0, 512))
                sq_cons = e_n
                E.dep("vector", e_n)
                e_r = vop(lambda e: e.reduce_max(A.ap("small", 0, 2, 0, 1),
                                                 psum[2][0:2, 0:512], axis=AX.X))
                BB[2] = e_r
                if cb == 0:
                    run_ev = vop(lambda e: e.tensor_copy(A.ap("small", 0, 2, 2, 3),
                                                         A.ap("small", 0, 2, 0, 1)))
                else:
                    run_ev = vop(lambda e: e.tensor_tensor(A.ap("small", 0, 2, 2, 3),
                                                           A.ap("small", 0, 2, 2, 3),
                                                           A.ap("small", 0, 2, 0, 1), AL.max))
            E.dep("tensor", run_ev)
            bdep(3)
            e_t = tr(psum[3][0:1, 0:2], A.ap("small", 0, 2, 2, 3))
            E.dep("vector", e_t)
            e_qm = vop(lambda e, j=j: e.tensor_copy(
                A.ap("qmk2", 0, 1, 16 + 2 * j, 16 + 2 * j + 2), psum[3][0:1, 0:2]))
            BB[3] = e_qm
            qm_done.append(e_qm)
        E.dep("scalar", *qm_done)
        e_lnK = sop(lambda e: e.activation(A.ap("qmk2", 0, 1, 16, 16 + HPR),
                                           A.ap("qmk2", 0, 1, 16, 16 + HPR), AF.Ln))

        for seg in range(NSEG):
            sbase = seg * (QSEG // 128)
            E.dep("sync", *wb_cons)
            wb_cons = []
            ev_wb = [ld(A.ap("wbuf", 0, 128, fi * 512, fi * 512 + HW),
                        wq2[fi * 128:(fi + 1) * 128, fbase:fbase + HW], stream="LW")
                     for fi in range(DC)]
            for ot in range(QSEG // 128):
                E.dep("gpsimd", hn_last[0], hn_last[1])
                e_x1 = ld_x1_own(A.ap("x1b"), sbase + ot)
                e_hn, e_hnsq = layer_norm(A.ap("x1b"), A.ap("hnt"), A.ap("PT", 0, 128, 0, D),
                                          g_ap=G1AP, b_ap=B1AP, deps=[e_x1])
                hn_last = (e_hn, e_hnsq)
                trev = {}
                for fc in range(DC):
                    E.dep("tensor", e_hn)
                    bdep(4 + fc % 2)
                    e_tp = tr(psum[4 + fc % 2][:, 0:128],
                              A.ap("hnt", 0, 128, fc * 128, fc * 128 + 128))
                    E.dep("vector", e_tp, hnT_cons)
                    e_hT = vop(lambda e, fc=fc: e.tensor_copy(
                        A.ap("hnT", 0, 128, fc * 128, fc * 128 + 128), psum[4 + fc % 2][:, 0:128]))
                    BB[4 + fc % 2] = e_hT
                    trev[fc] = e_hT
                for j in range(HT):
                    bi = j % 2
                    E.dep("tensor", *ev_wb)
                    bdep(bi)
                    e_mmq = None
                    for fi in range(DC):
                        E.dep("tensor", trev[fi])
                        e_mmq = mm(psum[bi][:, 0:128],
                                   A.ap("wbuf", 0, 128, fi * 512 + j * 128, fi * 512 + j * 128 + 128),
                                   A.ap("hnT", 0, 128, fi * 128, fi * 128 + 128),
                                   start=(fi == 0), stop=(fi == DC - 1))
                    wb_cons.append(e_mmq)
                    hnT_cons = e_mmq
                    E.dep("scalar", e_mmq)
                    e_evq = sop(lambda e, bi=bi, j=j, ot=ot, fbase=fbase: e.activation(
                        QT2ap(j, ot * 128, ot * 128 + 128), psum[bi][:, 0:128], AF.Identity,
                        bias=A.ap("b2c", 0, 128, (fbase // 128) + j, (fbase // 128) + j + 1)))
                    BB[bi] = e_evq

            qm_done = []
            for j in range(HT):
                run_ev = None
                ncb = (QSEG + 511) // 512
                for cb in range(ncb):
                    w = min(512, QSEG - cb * 512)
                    E.dep("scalar", sq_cons)
                    e_sq = sop(lambda e, j=j, cb=cb, w=w: e.activation(
                        A.ap("PT", 0, 128, 0, w), QT2ap(j, cb * 512, cb * 512 + w), AF.Square))
                    E.dep("tensor", e_sq)
                    bdep(2)
                    e_n = mm(psum[2][0:2, 0:w], A.ap("e2", 0, 128, 0, 2), A.ap("PT", 0, 128, 0, w))
                    sq_cons = e_n
                    E.dep("vector", e_n)
                    e_r = vop(lambda e, w=w: e.reduce_max(A.ap("small", 0, 2, 0, 1),
                                                          psum[2][0:2, 0:w], axis=AX.X))
                    BB[2] = e_r
                    if cb == 0:
                        run_ev = vop(lambda e: e.tensor_copy(A.ap("small", 0, 2, 2, 3),
                                                             A.ap("small", 0, 2, 0, 1)))
                    else:
                        run_ev = vop(lambda e: e.tensor_tensor(A.ap("small", 0, 2, 2, 3),
                                                               A.ap("small", 0, 2, 2, 3),
                                                               A.ap("small", 0, 2, 0, 1), AL.max))
                E.dep("tensor", run_ev)
                bdep(3)
                e_t = tr(psum[3][0:1, 0:2], A.ap("small", 0, 2, 2, 3))
                E.dep("vector", e_t)
                e_qm = vop(lambda e, j=j: e.tensor_copy(
                    A.ap("qmk2", 0, 1, 2 * j, 2 * j + 2), psum[3][0:1, 0:2]))
                BB[3] = e_qm
                qm_done.append(e_qm)
            E.dep("scalar", *qm_done, e_lnK)
            sop(lambda e: e.activation(A.ap("qmk2", 0, 1, 0, HPR), A.ap("qmk2", 0, 1, 0, HPR), AF.Ln))
            E.dep("vector", E.last_compute["scalar"])
            e_vadd2 = vop(lambda e: e.tensor_tensor(A.ap("qmk2", 0, 1, 0, HPR),
                                                    A.ap("qmk2", 0, 1, 0, HPR),
                                                    A.ap("qmk2", 0, 1, 16, 16 + HPR), AL.add))
            E.dep("scalar", e_vadd2)
            e_exp2 = sop(lambda e: e.activation(A.ap("qmk2", 0, 1, 0, HPR), A.ap("qmk2", 0, 1, 0, HPR),
                                                AF.Exp, scale=0.5, bias=LN8TH))
            E.dep("tensor", e_exp2)
            bdep(3)
            e_bc2 = mm(psum[3][:, 0:HPR], ONES[0:1, :], A.ap("qmk2", 0, 1, 0, HPR))
            E.dep("vector", e_bc2)
            e_negc2 = vop(lambda e: e.tensor_scalar(A.ap("negc2", 0, 128, 0, HPR), psum[3][:, 0:HPR],
                                                    -1.0, None, AL.mult))
            BB[3] = e_negc2

            pt_busy = {}
            rr_busy = None
            for qbl in range(NQBS):
                qb = seg * NQBS + qbl
                for jp in range(HT):
                    h0, h1 = 2 * jp, 2 * jp + 1
                    av_done = {}
                    for kc in range(KC):
                        e_exp = {}
                        for (m, hh, prow) in ((0, h0, 0), (1, h1, 64)):
                            bi = (2 * m) + (kc % 2)
                            E.dep("tensor", e_negc2)
                            bdep(bi)
                            e_s = mm(psum[bi][:, 0:QB],
                                     KT2ap(jp, kc * 128, kc * 128 + 128, prow, prow + 64),
                                     QT2ap(jp, qbl * QB, qbl * QB + QB, prow, prow + 64))
                            E.dep("scalar", e_s, pt_busy.get((m, kc % 2)))
                            e_e = sop(lambda e, bi=bi, m=m, kc=kc, hh=hh: e.activation(
                                A.ap("PT", 0, 128, (2 * m + kc % 2) * 512, (2 * m + kc % 2) * 512 + QB),
                                psum[bi][:, 0:QB], AF.Exp, scale=0.125,
                                bias=A.ap("negc2", 0, 128, hh, hh + 1)))
                            BB[bi] = e_e
                            e_exp[m] = e_e
                        for (m, hh) in ((0, h0), (1, h1)):
                            E.dep("tensor", e_exp[m])
                            if kc == 0:
                                bdep(4 + m)
                            e_av = mm(psum[4 + m][0:65, 0:QB],
                                      V2ap(kc, hh, 0, 65),
                                      A.ap("PT", 0, 128, (2 * m + kc % 2) * 512,
                                           (2 * m + kc % 2) * 512 + QB),
                                      start=(kc == 0), stop=(kc == KC - 1))
                            pt_busy[(m, kc % 2)] = e_av
                            av_done[m] = e_av
                    for (m, hh) in ((0, h0), (1, h1)):
                        E.dep("vector", av_done[m], rr_busy)
                        e_sr = vop(lambda e, m=m: e.tensor_copy(A.ap("Rr", 64, 65, 0, QB),
                                                                psum[4 + m][64:65, 0:QB]))
                        e_rc = vop(lambda e: e.reciprocal(A.ap("Rr", 64, 65, 0, QB),
                                                          A.ap("Rr", 64, 65, 0, QB)))
                        E.dep("tensor", e_rc)
                        bdep(6)
                        e_Rb = mm(psum[6][0:64, 0:QB], ONES[64:65, 0:64], A.ap("Rr", 64, 65, 0, QB))
                        rr_busy = e_Rb
                        E.dep("vector", e_Rb)
                        e_Rs = vop(lambda e: e.tensor_copy(A.ap("Rb", 0, 64, 0, QB), psum[6][0:64, 0:QB]))
                        BB[6] = e_Rs
                        E.dep("vector", att_busy)
                        e_at = vop(lambda e, m=m: e.tensor_tensor(A.ap("att", 0, 64, 0, QB),
                                                                  psum[4 + m][0:64, 0:QB],
                                                                  A.ap("Rb", 0, 64, 0, QB), AL.mult))
                        BB[4 + m] = e_at
                        E.dep("gpsimd", e_at)
                        e_st = st(attoT_hbm[fbase + hh * 64:fbase + hh * 64 + 64,
                                            qb * QB:qb * QB + QB],
                                  A.ap("att", 0, 64, 0, QB))
                        att_busy = e_st
                        atto_store_last = e_st
        E.barrier()
        BB.clear()

    A.free("KT2", "V2", "QT2", "wbuf", "hnT", "hnt", "x1b", "PT", "att", "Rr",
           "Rb", "b2c", "qmk2", "negc2", "lnb1", "bvbs")

    A.alloc("wo_r", DC * D)
    A.alloc("aTl", 256)
    A.alloc("x1b4", D)
    A.alloc("x2t", D)
    A.alloc("hn2t", D)
    A.alloc("junk4", D)
    A.alloc("gwb", DC * NE)
    A.alloc("gbbs", NE)
    A.alloc("onehotT", OWN)
    LGB = (NE + 7) & ~7
    A.alloc("lgb", 4 * LGB)
    A.alloc("hnTg", 256)

    A.alloc("lnb2", 2 * D)
    A.alloc("bobs", D)
    ld(A.ap("lnb2", 0, 128, 0, D), g2b[:, :], stream="LB")
    ld(A.ap("lnb2", 0, 128, D, 2 * D), b2lnb[:, :], stream="LB")
    e_ph4c = ld(A.ap("bobs", 0, 128, 0, D), bo2b[:, :], stream="LB")
    for _eng in ("tensor", "vector", "scalar", "gpsimd"):
        E.dep(_eng, e_ph4c)
    G2AP = A.ap("lnb2", 0, 128, 0, D)
    B2AP = A.ap("lnb2", 0, 128, D, 2 * D)
    ev_wo = [ld(A.ap("wo_r", 0, 128, fc * D, fc * D + D), wo2[fc * 128:(fc + 1) * 128, :],
                 stream="LW") for fc in range(DC)]
    ld(A.ap("gwb", 0, 128, 0, DC * NE).rearrange("p (c n) -> p c n", n=NE),
       gwD.rearrange("(c p) n -> p c n", p=128), stream="LW")
    ld(A.ap("gbbs", 0, 128, 0, NE), gbb[:, :], stream="LW")

    hn2_store_last = None
    x2_store_last = None
    x2t_st_prev = None
    hn2t_st_prev = None
    oh_done = []
    for i in range(OT):
        e_mo = {}
        aTl_cons = None
        for fc in range(DC):
            E.dep("sync", aTl_cons)
            if fc == 0 and i == 0:
                E.dep("sync", atto_store_last)
            e_a = ld(A.ap("aTl", 0, 128, (fc % 2) * 128, (fc % 2) * 128 + 128),
                     attoT_hbm[fc * 128:(fc + 1) * 128, i * 128:(i + 1) * 128],
                     stream=f"L{fc % 2}")
            for s in range(DH):
                E.dep("tensor", e_a, ev_wo[fc])
                if fc == 0:
                    bdep(s)
                e_mo[s] = mm(psum[s][:, 0:DW],
                             A.ap("aTl", 0, 128, (fc % 2) * 128, (fc % 2) * 128 + 128),
                             A.ap("wo_r", 0, 128, fc * D + s * DW, fc * D + s * DW + DW),
                             start=(fc == 0), stop=(fc == DC - 1))
            aTl_cons = e_mo[DH - 1]
        e_x1 = ld_x1_own(A.ap("x1b4"), i)
        ep = []
        for s in range(DH):
            E.dep("vector", e_mo[s], e_x1, x2t_st_prev)
            e_p = vop(lambda e, s=s: e.scalar_tensor_tensor(
                A.ap("x2t", 0, 128, s * DW, s * DW + DW), psum[s][:, 0:DW], 1.0,
                A.ap("x1b4", 0, 128, s * DW, s * DW + DW), AL.mult, AL.add))
            BB[s] = e_p
            ep.append(e_p)
        e_bo = vop(lambda e: e.tensor_tensor(A.ap("x2t"), A.ap("x2t"), A.ap("bobs"), AL.add))
        E.dep("gpsimd", e_bo)
        e_sx2 = st(x2_hbm[i * 128:(i + 1) * 128, :], A.ap("x2t"))
        x2_store_last = e_sx2
        e_hn2, e_hn2sq = layer_norm(A.ap("x2t"), A.ap("hn2t"), A.ap("junk4"),
                                    g_ap=G2AP, b_ap=B2AP,
                                    deps=[e_bo, hn2t_st_prev] if hn2t_st_prev else [e_bo])
        x2t_st_prev = e_sx2
        E.dep("gpsimd", e_hn2)
        e_sh2 = st(hn2_hbm[i * 128:(i + 1) * 128, :], A.ap("hn2t"))
        hn2_store_last = e_sh2
        hn2t_st_prev = e_sh2
        e_lg = None
        for fc in range(DC):
            E.dep("tensor", e_hn2)
            bdep(6)
            e_tp = tr(psum[6][:, 0:128], A.ap("hn2t", 0, 128, fc * 128, fc * 128 + 128))
            E.dep("vector", e_tp)
            e_hT = vop(lambda e, fc=fc: e.tensor_copy(
                A.ap("hnTg", 0, 128, (fc % 2) * 128, (fc % 2) * 128 + 128), psum[6][:, 0:128]))
            BB[6] = e_hT
            E.dep("tensor", e_hT)
            if fc == 0:
                bdep(7)
            e_lg = mm(psum[7][0:128, 0:NE],
                      A.ap("hnTg", 0, 128, (fc % 2) * 128, (fc % 2) * 128 + 128),
                      A.ap("gwb", 0, 128, fc * NE, fc * NE + NE),
                      start=(fc == 0), stop=(fc == DC - 1))
        E.dep("vector", e_lg)
        e_v1 = vop(lambda e: e.tensor_tensor(A.ap("lgb", 0, 128, 0, NE), psum[7][0:128, 0:NE],
                                             A.ap("gbbs", 0, 128, 0, NE), AL.add))
        BB[7] = e_v1
        vop(lambda e: e.reduce_max(A.ap("small", 0, 128, 8, 9), A.ap("lgb", 0, 128, 0, NE), axis=AX.X))
        vop(lambda e: e.tensor_scalar(A.ap("lgb", 0, 128, LGB, LGB + NE),
                                      A.ap("lgb", 0, 128, 0, NE),
                                      A.ap("small", 0, 128, 8, 9), None, AL.is_ge))
        vop(lambda e: e.memset(A.ap("lgb", 0, 128, 2 * LGB, 2 * LGB + NE), 1.0))
        vop(lambda e: e.tensor_tensor_scan(A.ap("lgb", 0, 128, 3 * LGB, 3 * LGB + NE),
                                           A.ap("lgb", 0, 128, 2 * LGB, 2 * LGB + NE),
                                           A.ap("lgb", 0, 128, LGB, LGB + NE),
                                           0.0, AL.mult, AL.add))
        vop(lambda e: e.tensor_scalar(A.ap("lgb", 0, 128, 3 * LGB, 3 * LGB + NE),
                                      A.ap("lgb", 0, 128, 3 * LGB, 3 * LGB + NE),
                                      1.0, None, AL.is_equal))
        e_oh = vop(lambda e: e.tensor_tensor(A.ap("lgb", 0, 128, LGB, LGB + NE),
                                             A.ap("lgb", 0, 128, LGB, LGB + NE),
                                             A.ap("lgb", 0, 128, 3 * LGB, 3 * LGB + NE), AL.mult))
        E.dep("tensor", e_oh)
        bdep(6)
        e_tp2 = tr(psum[6][0:NE, 0:128], A.ap("lgb", 0, 128, LGB, LGB + NE))
        E.dep("vector", e_tp2)
        e_ohT = vop(lambda e, i=i: e.tensor_copy(
            A.ap("onehotT", 0, NE, i * 128, (i + 1) * 128), psum[6][0:NE, 0:128]))
        BB[6] = e_ohT
        oh_done.append(e_ohT)
        E.barrier()

    A.free("wo_r", "aTl", "x1b4", "x2t", "hn2t", "junk4", "gwb", "gbbs", "lgb", "hnTg",
           "lnb2", "bobs")
    BB.clear()

    A.alloc("ones8", OWN)
    A.alloc("csT", OWN)
    A.alloc("destT", OWN)
    A.alloc("destc", (OT + 7) & ~7)
    AU.alloc("desti", (OT + 7) & ~7)
    AU.alloc("iotat", (OT + 7) & ~7)
    AU.alloc("invt", (SC + 7) & ~7)
    AU.alloc("uz", (SC + 7) & ~7)

    E.dep("vector", *oh_done)
    vop(lambda e: e.memset(A.ap("ones8", 0, NE, 0, OWN), 1.0))
    vop(lambda e: e.tensor_tensor_scan(A.ap("csT", 0, NE, 0, OWN), A.ap("ones8", 0, NE, 0, OWN),
                                       A.ap("onehotT", 0, NE, 0, OWN), 0.0, AL.mult, AL.add))
    vop(lambda e: e.tensor_tensor(A.ap("csT", 0, NE, 0, OWN), A.ap("csT", 0, NE, 0, OWN),
                                  A.ap("onehotT", 0, NE, 0, OWN), AL.subtract))
    e_dp = vop(lambda e: e.scalar_tensor_tensor(A.ap("csT", 0, NE, 0, OWN),
                                                A.ap("csT", 0, NE, 0, OWN),
                                                A.ap("basecol", 0, NE, 0, 1),
                                                A.ap("onehotT", 0, NE, 0, OWN), AL.add, AL.mult))
    dt_prev = None
    dt_done = []
    for cb in range(OWN // 512):
        E.dep("tensor", e_dp, dt_prev)
        e_d = mm(psum[0][0:1, 0:512], ONES[0:NE, 0:1], A.ap("csT", 0, NE, cb * 512, cb * 512 + 512))
        E.dep("vector", e_d)
        dt_prev = vop(lambda e, cb=cb: e.tensor_copy(
            A.ap("destT", 0, 1, cb * 512, cb * 512 + 512), psum[0][0:1, 0:512]))
        dt_done.append(dt_prev)
    if debug:
        E.dep("gpsimd", *dt_done)
        st(dbg_dest[:, :], A.ap("destT", 0, 1, 0, OWN))
    dc_prev = None
    dc_done = []
    for i in range(OT):
        E.dep("tensor", *dt_done)
        E.dep("tensor", dc_prev)
        e_t = tr(psum[1][0:128, 0:1], A.ap("destT", 0, 1, i * 128, i * 128 + 128))
        E.dep("vector", e_t)
        dc_prev = vop(lambda e, i=i: e.tensor_copy(A.ap("destc", 0, 128, i, i + 1),
                                                   psum[1][0:128, 0:1]))
        dc_done.append(dc_prev)
    E.dep("vector", *dc_done)
    e_cast = vop(lambda e: e.tensor_copy(AU.ap("desti", 0, 128, 0, OT), A.ap("destc", 0, 128, 0, OT)))
    e_iot = gop(lambda e: e.iota(AU.ap("iotat", 0, 128, 0, OT), [[128, OT]], base=0,
                                 channel_multiplier=1, allow_small_or_imprecise_dtypes=True))
    e_uz = gop(lambda e: e.memset(AU.ap("uz", 0, 128, 0, SC), 0))
    E.force_wait("gpsimd", e_uz)
    e_zf = st(invtok_hbm.rearrange("(c p) o -> p (c o)", p=128), AU.ap("uz", 0, 128, 0, SC), slow=True)
    sc_evs = []
    for i in range(OT):
        E.dep("gpsimd", e_cast)
        E.force_wait("gpsimd", e_zf, e_iot)
        sc_evs.append(gop(lambda e, i=i: e.indirect_dma_start(
            out=invtok_hbm[:, :], out_offset=IOA(AU.ap("desti", 0, 128, i, i + 1), 0),
            in_=AU.ap("iotat", 0, 128, i, i + 1), in_offset=None), amt=16))
    E.dep("sync", *sc_evs)
    e_ivt = ld(AU.ap("invt", 0, 128, 0, SC), invtok_hbm.rearrange("(c p) o -> p (c o)", p=128), slow=True)
    A.free("ones8", "csT", "destT", "destc")
    E.barrier()
    BB.clear()

    A.alloc("ghT", DC * CAP)
    A.alloc("Gt", CS * D)
    A.alloc("w1b", 2 * DC * 128)
    A.alloc("w2b", 2 * D)
    A.alloc("midb", 2 * CAP)
    A.alloc("b1c", FC)
    A.alloc("b2r", D)
    A.alloc("b2bc", D)
    A.alloc("yev", 2 * DW)

    def ghTap(fc, c0, c1):
        return A.ap("ghT", 0, 128, fc * CAP + c0, fc * CAP + c1)

    NYB = CS * DH
    assert NYB <= 6, "y accumulator banks exceed budget"
    y_store_last = None
    yev_busy = [None, None]
    for ex in range(NE):
        g_evs = []
        for cs_ in range(CS):
            E.dep("gpsimd", e_ivt)
            if ex == 0 and cs_ == 0:
                E.force_wait("gpsimd", hn2_store_last)
            g_evs.append(gop(lambda e, cs_=cs_, ex=ex: e.indirect_dma_start(
                out=A.ap("Gt", 0, 128, cs_ * D, cs_ * D + D), out_offset=None,
                in_=hn2_hbm[:, :],
                in_offset=IOA(AU.ap("invt", 0, 128, ex * CS + cs_, ex * CS + cs_ + 1), 0)), amt=16))
        trn_last = None
        for cs_ in range(CS):
            for fc in range(DC):
                par = (cs_ * DC + fc) % 2
                E.dep("tensor", g_evs[cs_])
                bdep(6 + par)
                e_tp = tr(psum[6 + par][:, 0:128], A.ap("Gt", 0, 128, cs_ * D + fc * 128,
                                                        cs_ * D + fc * 128 + 128))
                E.dep("vector", e_tp)
                trn_last = vop(lambda e, cs_=cs_, fc=fc, par=par: e.tensor_copy(
                    ghTap(fc, cs_ * 128, cs_ * 128 + 128), psum[6 + par][:, 0:128]))
                BB[6 + par] = trn_last
        e_b1 = ld(A.ap("b1c", 0, 128, 0, FC), b1m[ex, :].rearrange("(c p) -> p c", p=128), slow=True)
        e_b2r = ld(A.ap("b2r", 0, 1, 0, D), b2m[ex:ex + 1, :])
        bb_evs = []
        for s in range(DH):
            E.dep("tensor", e_b2r)
            bdep(6)
            e_bb = mm(psum[6][:, 0:DW], ONES[0:1, :], A.ap("b2r", 0, 1, s * DW, s * DW + DW))
            E.dep("vector", e_bb)
            e_bbc = vop(lambda e, s=s: e.tensor_copy(
                A.ap("b2bc", 0, 128, s * DW, s * DW + DW), psum[6][:, 0:DW]))
            BB[6] = e_bbc
            bb_evs.append(e_bbc)
        w1_busy = [None, None]
        w2_busy = [None, None]
        e_y = {}
        for f2c in range(FC):
            par = f2c % 2
            E.dep("sync", w1_busy[par])
            e_w1 = ld(A.ap("w1b", 0, 128, par * DC * 128, (par + 1) * DC * 128)
                      .rearrange("p (hc f) -> p hc f", f=128),
                      w1[ex, :, f2c * 128:(f2c + 1) * 128].rearrange("(hc p) f -> p hc f", p=128),
                      stream=f"L{par}")
            E.dep("sync", w2_busy[par])
            e_w2 = ld(A.ap("w2b", 0, 128, par * D, (par + 1) * D),
                      w2[ex, f2c * 128:(f2c + 1) * 128, :], stream=f"L{par}")
            E.dep("tensor", e_w1, trn_last)
            bdep(6 + par)
            e_mid = None
            for hc in range(DC):
                e_mid = mm(psum[6 + par][:, 0:CAP],
                           A.ap("w1b", 0, 128, par * DC * 128 + hc * 128,
                                par * DC * 128 + hc * 128 + 128),
                           ghTap(hc, 0, CAP), start=(hc == 0), stop=(hc == DC - 1))
            w1_busy[par] = e_mid
            E.dep("scalar", e_mid, e_b1)
            e_rl = sop(lambda e, par=par, f2c=f2c: e.activation(
                A.ap("midb", 0, 128, par * CAP, par * CAP + CAP), psum[6 + par][:, 0:CAP],
                AF.Relu, bias=A.ap("b1c", 0, 128, f2c, f2c + 1)))
            BB[6 + par] = e_rl
            last_y = None
            for si in range(CS):
                for s in range(DH):
                    bi = si * DH + s
                    E.dep("tensor", e_rl, e_w2)
                    if f2c == 0:
                        E.dep("tensor", *bb_evs)
                        bdep(bi)
                    last_y = mm(psum[bi][:, 0:DW],
                                A.ap("midb", 0, 128, par * CAP + si * 128,
                                     par * CAP + si * 128 + 128),
                                A.ap("w2b", 0, 128, par * D + s * DW, par * D + s * DW + DW),
                                start=(f2c == 0), stop=(f2c == FC - 1))
                    e_y[bi] = last_y
            w2_busy[par] = last_y
        for si in range(CS):
            for s in range(DH):
                bi = si * DH + s
                par = bi % 2
                E.dep("vector", e_y[bi], yev_busy[par])
                e_ye = vop(lambda e, bi=bi, par=par, s=s: e.scalar_tensor_tensor(
                    A.ap("yev", 0, 128, par * DW, par * DW + DW), psum[bi][:, 0:DW], 1.0,
                    A.ap("b2bc", 0, 128, s * DW, s * DW + DW), AL.mult, AL.add))
                BB[bi] = e_ye
                E.dep("gpsimd", e_ye)
                e_sy = st(Y_hbm[ex * CAP + si * 128:ex * CAP + si * 128 + 128,
                                s * DW:s * DW + DW],
                          A.ap("yev", 0, 128, par * DW, par * DW + DW))
                yev_busy[par] = e_sy
                y_store_last = e_sy
        E.barrier()
        BB.clear()

    A.free("ghT", "Gt", "w1b", "w2b", "midb", "b1c", "b2r", "b2bc", "yev")

    A.alloc("mo", D)
    A.alloc("x1b7", D)
    A.alloc("x2b", D)
    A.alloc("tt", D)
    A.alloc("junk7", D)
    last_store = None
    for i in range(OT):
        if i == 0:
            E.force_wait("gpsimd", y_store_last)
        e_mo = gop(lambda e, i=i: e.indirect_dma_start(
            out=A.ap("mo"), out_offset=None, in_=Y_hbm[:, :],
            in_offset=IOA(AU.ap("desti", 0, 128, i, i + 1), 0)), amt=16)
        e_x1 = ld_x1_own(A.ap("x1b7"), i)
        if i == 0:
            E.dep("sync", x2_store_last)
        e_x2 = ld(A.ap("x2b"), x2_hbm[i * 128:(i + 1) * 128, :])
        E.dep("vector", e_x1, e_x2)
        e_t1 = vop(lambda e: e.tensor_tensor(A.ap("tt"), A.ap("x1b7"), A.ap("x2b"), AL.add))
        E.dep("vector", e_mo)
        e_t2 = vop(lambda e: e.tensor_tensor(A.ap("tt"), A.ap("tt"), A.ap("mo"), AL.add))
        e_ln, _ = layer_norm(A.ap("tt"), A.ap("tt"), A.ap("junk7"), deps=[e_t2])
        E.dep("gpsimd", e_ln)
        last_store = st(y_out[i * 128:(i + 1) * 128, :], A.ap("tt"))
        E.barrier()
    E.force_wait("gpsimd", last_store)

    E.assign_values()
    with nc.Block() as block:
        @block.sync
        def _(h):
            E.emit_engine("sync", h, sems)

        @block.tensor
        def _(h):
            E.emit_engine("tensor", h, sems)

        @block.vector
        def _(h):
            E.emit_engine("vector", h, sems)

        @block.scalar
        def _(h):
            E.emit_engine("scalar", h, sems)

        @block.gpsimd
        def _(h):
            E.emit_engine("gpsimd", h, sems)

    ctx.close()
    nc._dbgA, nc._dbgAU = A, AU
    return nc



K_CONSTS = CONST_VALUES


def host_constants(C):
    T, D, NE, CAP = C.T, C.D, C.NE, C.CAP
    ident = np.eye(128, dtype=np.float32)
    tri = np.zeros((128, 128), dtype=np.float32)
    for p in range(128):
        tri[p, p:] = NEGBIG
    ones = np.ones((128, 128), dtype=np.float32)
    e2 = np.zeros((128, 2), dtype=np.float32)
    e2[0:64, 0] = 1.0
    e2[64:128, 1] = 1.0
    basecol = (np.arange(NE, dtype=np.float32) * CAP).reshape(NE, 1)
    return ident, tri, ones, e2, basecol


def make_core_inputs(C, hs_b, u_b, ipw, ipb, ffn, half=0):
    T, D = C.T, C.D
    ident, tri, ones, e2, basecol = host_constants(C)
    bcast = lambda v: np.ascontiguousarray(np.broadcast_to(np.asarray(v, np.float32), (128, D)))
    bcN = lambda v: np.ascontiguousarray(np.broadcast_to(np.asarray(v, np.float32), (128, C.NE)))
    d = {
        "x0": np.ascontiguousarray(hs_b),
        "x0T": np.ascontiguousarray(hs_b.T),
        "u": np.ascontiguousarray(u_b),
        "wq1T": np.ascontiguousarray(ipw[0:D].T),
        "wk1T": np.ascontiguousarray(ipw[D:2 * D].T),
        "bq1": np.ascontiguousarray(ipb[0:D]),
        "bk1": np.ascontiguousarray(ipb[D:2 * D]),
        "wq2": np.ascontiguousarray(ffn["wq"]),
        "wk2": np.ascontiguousarray(ffn["wk"]),
        "wv2": np.ascontiguousarray(ffn["wv"]),
        "wo2": np.ascontiguousarray(ffn["wo"]),
        "bq2": np.ascontiguousarray(ffn["bq"]),
        "bk2": np.ascontiguousarray(ffn["bk"]),
        "bv2b": bcast(ffn["bv"]),
        "bo2b": bcast(ffn["bo"]),
        "g1b": bcast(ffn["ln1_g"]),
        "b1lnb": bcast(ffn["ln1_b"]),
        "g2b": bcast(ffn["ln2_g"]),
        "b2lnb": bcast(ffn["ln2_b"]),
        "gw": np.ascontiguousarray(ffn["gate_w"]),
        "gbb": bcN(ffn["gate_b"]),
        "w1": np.ascontiguousarray(ffn["w1"]),
        "b1m": np.ascontiguousarray(ffn["b1"]),
        "w2": np.ascontiguousarray(ffn["w2"]),
        "b2m": np.ascontiguousarray(ffn["b2"]),
        "cpool": np.ascontiguousarray(
            np.broadcast_to(np.array(list(K_CONSTS) + [0.0] * (16 - len(K_CONSTS)),
                                     np.float32), (128, 16))),
        "ident": ident,
        "trimask": tri,
        "ones": ones,
        "e2": e2,
        "basecol": basecol,
        "ownidx": (np.arange(C.T // 2, dtype=np.uint32) + half * (C.T // 2)).reshape(-1, 1),
    }
    return d


def kernel(
    L0_attn_hidden_states, L0_attn_in_proj_weight, L0_attn_in_proj_bias,
    L0_ffn_ln1_g, L0_ffn_ln1_b, L0_ffn_ln2_g, L0_ffn_ln2_b,
    L0_ffn_wq, L0_ffn_bq, L0_ffn_wk, L0_ffn_bk, L0_ffn_wv, L0_ffn_bv,
    L0_ffn_wo, L0_ffn_bo, L0_ffn_gate_w, L0_ffn_gate_b,
    L0_ffn_w1, L0_ffn_b1, L0_ffn_w2, L0_ffn_b2,
):
    C = FULL
    T, D = C.T, C.D
    hs = np.asarray(L0_attn_hidden_states, dtype=np.float32)
    B = hs.shape[0]

    import jax
    with jax.default_device(jax.devices("cpu")[0]):
        _u = jax.random.uniform(
            jax.random.key(42, impl="threefry2x32"), (B, T, T),
            jax.numpy.float32, 1e-06, 1.0 - 1e-06)
        u_all = np.asarray(-jax.numpy.log(-jax.numpy.log(_u)))

    ffn = {
        "wq": L0_ffn_wq, "wk": L0_ffn_wk, "wv": L0_ffn_wv, "wo": L0_ffn_wo,
        "bq": L0_ffn_bq, "bk": L0_ffn_bk, "bv": L0_ffn_bv, "bo": L0_ffn_bo,
        "ln1_g": L0_ffn_ln1_g, "ln1_b": L0_ffn_ln1_b,
        "ln2_g": L0_ffn_ln2_g, "ln2_b": L0_ffn_ln2_b,
        "gate_w": L0_ffn_gate_w, "gate_b": L0_ffn_gate_b,
        "w1": L0_ffn_w1, "b1": L0_ffn_b1, "w2": L0_ffn_w2, "b2": L0_ffn_b2,
    }
    ffn = {k: np.asarray(v, dtype=np.float32) for k, v in ffn.items()}
    ipw = np.asarray(L0_attn_in_proj_weight, dtype=np.float32)
    ipb = np.asarray(L0_attn_in_proj_bias, dtype=np.float32)

    nc = build_program(C)
    in_maps = []
    shared = {}
    for c in range(8):
        b, half = c // 2, c % 2
        if (b, half) not in shared:
            shared[(b, half)] = make_core_inputs(C, hs[b], u_all[b], ipw, ipb, ffn, half=half)
        in_maps.append(shared[(b, half)])

    res = run_bass_kernel_spmd(nc, in_maps, core_ids=list(range(8)))

    OWN = T // 2
    out = np.empty((B, T, D), dtype=np.float32)
    for c in range(8):
        b, half = c // 2, c % 2
        out[b, half * OWN:(half + 1) * OWN, :] = np.asarray(res.results[c]["y"])
    return out


# version 10 (final)
# speedup vs baseline: 1.0534x; 1.0534x over previous
import math
from contextlib import ExitStack

import numpy as np

import concourse.bass as bass
import concourse.mybir as mybir
from concourse.bass_utils import run_bass_kernel_spmd

F32 = mybir.dt.float32
U32 = mybir.dt.uint32
AF = mybir.ActivationFunctionType
AL = mybir.AluOpType
AX = mybir.AxisListType
IOA = bass.IndirectOffsetOnAxis

NEGBIG = -1.0e9
EPS = 1e-5
LN8TH = math.log(0.125)
CONST_VALUES = (0.0, 1.0, 0.125, 0.5, -0.5, -1.0, 16.0, 1e-6, EPS, LN8TH)

ENGINES = ("sync", "tensor", "vector", "scalar", "gpsimd")
OWN_SEM = {"sync": "L", "tensor": "T", "vector": "V", "scalar": "S", "gpsimd": "G"}
ALL_SEMS = ("LB", "LW", "LS", "LU", "LX", "L0", "L1", "LC", "T", "V", "S", "G", "GD")


class Cfg:
    def __init__(self, **kw):
        self.__dict__.update(kw)
        assert self.D % 128 == 0 and self.D == self.NH * 64
        assert self.T % 512 == 0 and self.F % 256 == 0 and self.CAP % 128 == 0
        assert self.GW % 128 == 0 and self.T % self.GW == 0
        assert (self.T // 2) % self.QB == 0 and self.QB <= 512
        assert self.NH % 4 == 0


FULL = Cfg(T=4096, D=1024, NH=16, F=4096, NE=8, CAP=384, GW=256, QB=512)


class Em:

    CHAINED = ("vector", "scalar", "gpsimd")

    def __init__(self):
        self.streams = {e: [] for e in ENGINES}
        self.events = []
        self.latest_dma = {}
        self.last_compute = {e: None for e in ENGINES}
        self.last_by_sem = {}

    def op(self, eng, build, amt=1, sem=None):
        eid = len(self.events)
        if sem is None:
            sem = {"sync": "LC", "gpsimd": "GD" if amt == 16 else "G"}.get(
                eng, OWN_SEM[eng])
        self.events.append([eng, amt, False, sem])
        self.last_by_sem[sem] = eid
        if amt == 16:
            self.latest_dma[sem] = eid
        else:
            if eng in self.CHAINED:
                prev = self.last_compute[eng]
                if prev is not None:
                    self.events[prev][2] = True
                    self.streams[eng].append(("wait", prev, None))
                self.last_compute[eng] = eid
        self.streams[eng].append(("op", build, eid))
        return eid

    def _upgrade(self, eid):
        ev = self.events[eid]
        if ev[1] == 16:
            return self.latest_dma[ev[3]]
        return eid

    def dep(self, eng, *eids):
        for eid in eids:
            if eid is None:
                continue
            eid = self._upgrade(eid)
            self.events[eid][2] = True
            self.streams[eng].append(("wait", eid, None))

    def force_wait(self, eng, *eids):
        for eid in eids:
            if eid is None:
                continue
            eid = self._upgrade(eid)
            self.events[eid][2] = True
            self.streams[eng].append(("wait", eid, None))

    def barrier(self):
        evs = list(self.last_by_sem.values())
        for e in ENGINES:
            self.dep(e, *evs)

    def assign_values(self):
        for ev in self.events:
            if ev[1] == 16:
                ev[2] = True
        cnt = {s: 0 for s in ALL_SEMS}
        self.val = {}
        for eng in ENGINES:
            for kind, a, b in self.streams[eng]:
                if kind == "op" and self.events[b][2]:
                    s = self.events[b][3]
                    cnt[s] += self.events[b][1]
                    self.val[b] = cnt[s]

    def emit_engine(self, eng, h, sems):
        waited = {s: 0 for s in ALL_SEMS}
        for kind, a, b in self.streams[eng]:
            if kind == "op":
                inst = a(h)
                if self.events[b][2]:
                    inst.then_inc(sems[self.events[b][3]], self.events[b][1])
            else:
                s = self.events[a][3]
                v = self.val[a]
                if waited[s] < v:
                    waited[s] = v
                    h.wait_ge(sems[s], v)


class Arena:
    def __init__(self, words):
        self.words = words
        self.live = {}
        self.frozen = {}
        self.top = 0
        self.peak = 0
        self.handle = None

    def alloc(self, name, words):
        words = (words + 7) & ~7
        assert name not in self.frozen, f"arena name reused: {name}"
        self.live[name] = (self.top, words)
        self.frozen[name] = (self.top, words)
        self.top += words
        self.peak = max(self.peak, self.top)
        if self.top > self.words:
            raise MemoryError(f"arena overflow at {name}: {self.top} > {self.words}")

    def free(self, *names):
        for n in names:
            self.live.pop(n)
        self.top = max([o + w for o, w in self.live.values()], default=0)

    def ap(self, name, p0=0, p1=128, c0=0, c1=None):
        off, w = self.frozen[name]
        if c1 is None:
            c1 = w
        assert 0 <= c0 <= c1 <= w, (name, c0, c1, w)
        return self.handle[p0:p1, off + c0:off + c1]


def build_program(C, debug=False):
    nc = bass.Bass("TRN2", target_bir_lowering=False)
    T, D, NH, F, NE, CAP = C.T, C.D, C.NH, C.F, C.NE, C.CAP
    GW, QB = C.GW, C.QB
    DC = D // 128
    TT = T // 128
    OWN = T // 2
    OT = OWN // 128
    HPR = NH // 2
    HT = HPR // 2
    HW = HPR * 64
    SC = (NE * CAP) // 128
    FC = F // 128
    GT = GW // 128
    NQB = OWN // QB
    KC = TT
    CS = CAP // 128
    DH = max(1, D // 512)
    DW = min(512, D)

    scratch_kind = "ExternalOutput" if debug else "Internal"

    def din(name, shape, dtype=F32):
        return nc.dram_tensor(name, shape, dtype, kind="ExternalInput").ap()

    x0 = din("x0", [T, D])
    x0T = din("x0T", [D, T])
    u_in = din("u", [T, T])
    wq1T = din("wq1T", [D, D])
    wk1T = din("wk1T", [D, D])
    bq1 = din("bq1", [D])
    bk1 = din("bk1", [D])
    wq2 = din("wq2", [D, D])
    wk2 = din("wk2", [D, D])
    wv2 = din("wv2", [D, D])
    wo2 = din("wo2", [D, D])
    bq2 = din("bq2", [D])
    bk2 = din("bk2", [D])
    bv2b = din("bv2b", [128, D])
    bo2b = din("bo2b", [128, D])
    g1b = din("g1b", [128, D])
    b1lnb = din("b1lnb", [128, D])
    g2b = din("g2b", [128, D])
    b2lnb = din("b2lnb", [128, D])
    gwD = din("gw", [D, NE])
    gbb = din("gbb", [128, NE])
    w1 = din("w1", [NE, D, F])
    b1m = din("b1m", [NE, F])
    w2 = din("w2", [NE, F, D])
    b2m = din("b2m", [NE, D])
    cpool_in = din("cpool", [128, 16])
    ident_in = din("ident", [128, 128])
    trimask_in = din("trimask", [128, 128])
    ones_in = din("ones", [128, 128])
    e2_in = din("e2", [128, 2])
    basecol_in = din("basecol", [NE, 1])
    ownidx_in = din("ownidx", [OWN, 1], U32)

    y_out = nc.dram_tensor("y", [OWN, D], F32, kind="ExternalOutput").ap()

    x1_hbm = nc.dram_tensor("x1_hbm", [T, D], F32, kind=scratch_kind).ap()
    x2_hbm = nc.dram_tensor("x2_hbm", [OWN, D], F32, kind=scratch_kind).ap()
    hn2_hbm = nc.dram_tensor("hn2_hbm", [OWN, D], F32, kind=scratch_kind).ap()
    attoT_hbm = nc.dram_tensor("attoT_hbm", [D, OWN], F32, kind=scratch_kind).ap()
    Y_hbm = nc.dram_tensor("Y_hbm", [NE * CAP, D], F32, kind=scratch_kind).ap()
    invtok_hbm = nc.dram_tensor("invtok_hbm", [NE * CAP, 1], U32, kind=scratch_kind).ap()
    if debug:
        dbg_dest = nc.dram_tensor("dbg_dest", [1, OWN], F32, kind="ExternalOutput").ap()

    E = Em()
    A = Arena(words=52900)
    AU = Arena(words=3 * ((OT + 7) & ~7) + 2 * ((SC + 7) & ~7) + 16)
    ctx = ExitStack()
    A.handle = ctx.enter_context(nc.sbuf_tensor("arena", [128, A.words], F32))
    AU.handle = ctx.enter_context(nc.sbuf_tensor("uarena", [128, AU.words], U32))
    psum = [ctx.enter_context(nc.psum_tensor(f"pb{i}", [128, 512], F32)) for i in range(8)]
    sems = {s: ctx.enter_context(nc.semaphore(f"sem{s}")) for s in ALL_SEMS}

    def mm(out_ap, lhsT_ap, rhs_ap, start=True, stop=True):
        return E.op("tensor", lambda e: e.matmul(out_ap, lhsT_ap, rhs_ap, start=start, stop=stop))

    def ld(out_ap, in_ap, slow=False, stream="LS"):
        return E.op("sync", lambda e: e.dma_start(out=out_ap, in_=in_ap,
                                                  allow_slow_non_contiguous=slow),
                    amt=16, sem=stream)

    def st(out_ap, in_ap, slow=False):
        return E.op("gpsimd", lambda e: e.dma_start(out=out_ap, in_=in_ap,
                                                    allow_slow_non_contiguous=slow), amt=16)

    vop = lambda b: E.op("vector", b)
    sop = lambda b: E.op("scalar", b)
    gop = lambda b, amt=1: E.op("gpsimd", b, amt=amt)

    BB = {}

    def bdep(*bis):
        E.dep("tensor", *[BB.get(b) for b in bis])

    AU.alloc("ownidx", (OT + 7) & ~7)

    def ld_x1_own(out_ap, i):
        return E.op("gpsimd", lambda e: e.indirect_dma_start(
            out=out_ap, out_offset=None, in_=x1_hbm[:, :],
            in_offset=IOA(AU.ap("ownidx", 0, 128, i, i + 1), 0)), amt=16)

    for nm, w in (("ident", 128), ("trimask", 128), ("ones", 128), ("e2", 8),
                  ("basecol", 8), ("stat", 8), ("small", 32), ("bq1c", DC),
                  ("bk1c", DC)):
        A.alloc(nm, w)
    A.alloc("cpool", 16)
    ld(A.ap("cpool"), cpool_in[:], stream="LB")
    for ci, cv in enumerate(CONST_VALUES):
        nc.const_aps.register(F32, cv, A.ap("cpool", 0, 128, ci, ci + 1))
    ld(A.ap("ident"), ident_in[:], stream="LB")
    ld(A.ap("trimask"), trimask_in[:], stream="LB")
    ld(A.ap("ones"), ones_in[:], stream="LB")
    ld(A.ap("e2", 0, 128, 0, 2), e2_in[:], stream="LB")
    ld(A.ap("basecol", 0, NE, 0, 1), basecol_in[:], stream="LB")
    ld(A.ap("bq1c", 0, 128, 0, DC), bq1.rearrange("(c p) -> p c", p=128), slow=True, stream="LB")
    ld(A.ap("bk1c", 0, 128, 0, DC), bk1.rearrange("(c p) -> p c", p=128), slow=True, stream="LB")
    e_consts = ld(AU.ap("ownidx", 0, 128, 0, OT),
                  ownidx_in.rearrange("(c p) o -> p (c o)", p=128), slow=True, stream="LB")
    IDN = A.ap("ident")
    TRI = A.ap("trimask")
    ONES = A.ap("ones")

    def tr(out_ap, in_ap):
        p = in_ap.shape[0]
        return E.op("tensor", lambda e: e.transpose(out_ap, in_ap, A.ap("ident", 0, p, 0, p)))

    for _eng in ("tensor", "vector", "scalar", "gpsimd"):
        E.dep(_eng, e_consts)
    E.barrier()

    def layer_norm(t_ap, out_ap, junk_ap, g_ap=None, b_ap=None, deps=()):
        s_sum = A.ap("stat", 0, 128, 0, 1)
        s_ssq = A.ap("stat", 0, 128, 1, 2)
        s_mean = A.ap("stat", 0, 128, 2, 3)
        s_var = A.ap("stat", 0, 128, 3, 4)
        s_rstd = A.ap("stat", 0, 128, 4, 5)
        E.dep("vector", *deps)
        vop(lambda e: e.reduce_sum(s_sum, t_ap, axis=AX.X))
        E.dep("scalar", *deps)
        ev_sq = sop(lambda e: e.activation(junk_ap, t_ap, AF.Square, accum_out=s_ssq))
        E.dep("vector", ev_sq)
        vop(lambda e: e.tensor_scalar(s_mean, s_sum, 1.0 / D, None, AL.mult))
        vop(lambda e: e.tensor_tensor(s_var, s_mean, s_mean, AL.mult))
        ev5 = vop(lambda e: e.scalar_tensor_tensor(s_var, s_ssq, 1.0 / D, s_var, AL.mult, AL.subtract))
        E.dep("scalar", ev5)
        sop(lambda e: e.activation(s_rstd, s_var, AF.Ln, bias=EPS))
        ev7 = sop(lambda e: e.activation(s_rstd, s_rstd, AF.Exp, scale=-0.5))
        E.dep("vector", ev7)
        ev8 = vop(lambda e: e.tensor_scalar(out_ap, t_ap, s_mean, s_rstd, AL.subtract, AL.mult))
        if g_ap is not None:
            vop(lambda e: e.tensor_tensor(out_ap, out_ap, g_ap, AL.mult))
            ev8 = vop(lambda e: e.tensor_tensor(out_ap, out_ap, b_ap, AL.add))
        return ev8, ev_sq

    A.alloc("KT", DC * T)

    def KTap(fc, c0, c1, p0=0, p1=128):
        return A.ap("KT", p0, p1, fc * T + c0, fc * T + c1)

    nPB = T // 512
    A.alloc("lnmkb", nPB * NH)
    A.alloc("wfull", DC * D)
    A.alloc("x0Tg", DC * 512)

    def wfap(fi, c0, c1):
        return A.ap("wfull", 0, 128, fi * D + c0, fi * D + c1)

    ev_wk = [ld(wfap(fi, 0, D), wk1T[fi * 128:(fi + 1) * 128, :], stream="LB") for fi in range(DC)]

    TB = 512
    kt_done = {}
    xg_cons = []
    for tb in range(T // TB):
        E.dep("sync", *xg_cons)
        xg_cons = []
        evs_x = [ld(A.ap("x0Tg", 0, 128, fi * TB, fi * TB + TB),
                    x0T[fi * 128:(fi + 1) * 128, tb * TB:(tb + 1) * TB], stream="LB") for fi in range(DC)]
        for fo in range(DC):
            bank = fo % 2
            E.dep("tensor", *evs_x, *ev_wk)
            bdep(bank)
            e_mm = None
            for fi in range(DC):
                e_mm = mm(psum[bank][:, 0:TB], wfap(fi, fo * 128, fo * 128 + 128),
                          A.ap("x0Tg", 0, 128, fi * TB, fi * TB + TB),
                          start=(fi == 0), stop=(fi == DC - 1))
            xg_cons.append(e_mm)
            E.dep("scalar", e_mm)
            e_ev = sop(lambda e, b=bank, fo=fo, tb=tb: e.activation(
                KTap(fo, tb * TB, tb * TB + TB), psum[b][:, 0:TB], AF.Identity,
                bias=A.ap("bk1c", 0, 128, fo, fo + 1)))
            BB[bank] = e_ev
            kt_done[fo] = e_ev

    A.alloc("sqb", 512)
    A.alloc("mkrows", nPB * NH)
    mk_done = []
    sqb_cons = None
    for fc in range(DC):
        run_ev = None
        for cb in range(nPB):
            E.dep("scalar", kt_done[fc], sqb_cons)
            e_sq = sop(lambda e, fc=fc, cb=cb: e.activation(
                A.ap("sqb"), KTap(fc, cb * 512, cb * 512 + 512), AF.Square))
            E.dep("tensor", e_sq)
            bdep(2)
            e_n = mm(psum[2][0:2, 0:512], A.ap("e2", 0, 128, 0, 2), A.ap("sqb"))
            sqb_cons = e_n
            E.dep("vector", e_n)
            e_r = vop(lambda e: e.reduce_max(A.ap("small", 0, 2, 0, 1), psum[2][0:2, 0:512], axis=AX.X))
            BB[2] = e_r
            if cb == 0:
                run_ev = vop(lambda e: e.tensor_copy(A.ap("small", 0, 2, 2, 3), A.ap("small", 0, 2, 0, 1)))
            else:
                run_ev = vop(lambda e: e.tensor_tensor(A.ap("small", 0, 2, 2, 3),
                                                       A.ap("small", 0, 2, 2, 3),
                                                       A.ap("small", 0, 2, 0, 1), AL.max))
            E.dep("tensor", run_ev)
            bdep(3)
            e_t = tr(psum[3][0:1, 0:2], A.ap("small", 0, 2, 2, 3))
            E.dep("vector", e_t)
            e_cpy = vop(lambda e, fc=fc, cb=cb: e.tensor_copy(
                A.ap("mkrows", 0, 1, cb * NH + 2 * fc, cb * NH + 2 * fc + 2), psum[3][0:1, 0:2]))
            BB[3] = e_cpy
            mk_done.append(e_cpy)
    E.dep("scalar", *mk_done)
    e_lnmk = sop(lambda e: e.activation(A.ap("mkrows", 0, 1, 0, nPB * NH),
                                        A.ap("mkrows", 0, 1, 0, nPB * NH), AF.Ln))
    for pb in range(nPB):
        E.dep("tensor", e_lnmk)
        bdep(3)
        e_bc = mm(psum[3][:, 0:NH], ONES[0:1, :], A.ap("mkrows", 0, 1, pb * NH, pb * NH + NH))
        E.dep("vector", e_bc)
        BB[3] = vop(lambda e, pb=pb: e.tensor_copy(A.ap("lnmkb", 0, 128, pb * NH, pb * NH + NH),
                                                   psum[3][:, 0:NH]))
    A.free("sqb", "mkrows", "x0Tg", "wfull")
    E.barrier()
    BB.clear()

    HSTR = D + 8
    A.alloc("avg", T)
    A.alloc("PA", T)
    A.alloc("PB", max(T, 3 * HSTR))
    A.alloc("QTg", DC * GW)
    A.alloc("x0Tq", DC * GW)
    WSW = min(D, 4 * 128)
    A.alloc("wslab", WSW)
    A.alloc("sch", 16)
    A.alloc("negc", NH)
    A.alloc("qm2", NH)
    A.alloc("ush", 512)
    A.alloc("tsh", 512)
    A.alloc("dwT", 256)
    A.alloc("x1t", D)

    def QTg_ap(fc, c0, c1, p0=0, p1=128):
        return A.ap("QTg", p0, p1, fc * GW + c0, fc * GW + c1)

    def hid_ap(par, c0, c1):
        return A.ap("PB", 0, 128, par * HSTR + c0, par * HSTR + c1)

    def x0t_ap(c0=0, c1=None):
        return A.ap("PB", 0, 128, 2 * HSTR + c0, 2 * HSTR + (c1 if c1 is not None else D))

    qtg_cons = []
    x1_store_last = None
    x1t_busy = None
    pa_scal_dep = None
    pb_scal_dep = []
    avg_scal_dep = None
    ush_cons = []
    prev_p2 = None
    nwave = max(1, DC // 4)
    per_wave = DC // nwave

    for g in range(T // GW):
        E.dep("sync", *qtg_cons)
        qtg_cons = []
        evs_x = [ld(A.ap("x0Tq", 0, 128, fi * GW, fi * GW + GW),
                    x0T[fi * 128:(fi + 1) * 128, g * GW:g * GW + GW], stream="LB") for fi in range(DC)]
        qt_ev = {}
        slab_cons = None
        for wave in range(nwave):
            fos = list(range(wave * per_wave, (wave + 1) * per_wave))
            fo0 = fos[0]
            for sl in range(DC):
                e_ws = []
                for i, fo in enumerate(fos):
                    E.dep("sync", slab_cons[i] if slab_cons else None)
                    e_ws.append(ld(A.ap("wslab", 0, 128, i * 128, i * 128 + 128),
                                   wq1T[sl * 128:(sl + 1) * 128, fo * 128:fo * 128 + 128]))
                slab_mms = []
                for i, fo in enumerate(fos):
                    bi = 4 + (i % 4)
                    E.dep("tensor", e_ws[i], evs_x[sl])
                    if sl == 0:
                        bdep(bi)
                    e_mm = mm(psum[bi][:, 0:GW],
                              A.ap("wslab", 0, 128, (fo - fo0) * 128, (fo - fo0) * 128 + 128),
                              A.ap("x0Tq", 0, 128, sl * GW, sl * GW + GW),
                              start=(sl == 0), stop=(sl == DC - 1))
                    if sl == DC - 1:
                        E.dep("scalar", e_mm)
                        qt_ev[fo] = sop(lambda e, bi=bi, fo=fo: e.activation(
                            QTg_ap(fo, 0, GW), psum[bi][:, 0:GW], AF.Identity,
                            bias=A.ap("bq1c", 0, 128, fo, fo + 1)))
                        BB[bi] = qt_ev[fo]
                    slab_mms.append(e_mm)
                slab_cons = slab_mms
        for ti in range(GT):
            gi = g * GT + ti
            ext = (gi + 1) * 128
            chunks = []
            off = 0
            while off < ext:
                w = min(512, ext - off)
                chunks.append((off, w))
                off += w
            dcol = ext - 128
            pb = (ext - 1) // 512

            E.dep("sync", prev_p2, avg_scal_dep)
            uld_evs = [ld(A.ap("avg", 0, 128, off, off + w),
                          u_in[gi * 128:(gi + 1) * 128, off:off + w], stream="LU")
                       for (off, w) in chunks]

            qn_done = []
            tsh_cons2 = None
            for fc in range(DC):
                E.dep("scalar", qt_ev[fc], *ush_cons)
                ush_cons = []
                e_sq = sop(lambda e, fc=fc, ti=ti: e.activation(
                    A.ap("ush", 0, 128, 0, 128), QTg_ap(fc, ti * 128, ti * 128 + 128), AF.Square))
                E.dep("tensor", e_sq)
                bdep(2)
                e_n = mm(psum[2][0:2, 0:128], A.ap("e2", 0, 128, 0, 2), A.ap("ush", 0, 128, 0, 128))
                ush_cons.append(e_n)
                E.dep("vector", e_n, tsh_cons2)
                e_cp = vop(lambda e: e.tensor_copy(A.ap("tsh", 0, 2, 0, 128), psum[2][0:2, 0:128]))
                BB[2] = e_cp
                E.dep("tensor", e_cp)
                bdep(3)
                e_t = tr(psum[3][0:128, 0:2], A.ap("tsh", 0, 2, 0, 128))
                tsh_cons2 = e_t
                E.dep("vector", e_t)
                e_q = vop(lambda e, fc=fc: e.tensor_copy(A.ap("qm2", 0, 128, 2 * fc, 2 * fc + 2),
                                                         psum[3][0:128, 0:2]))
                BB[3] = e_q
                qn_done.append(e_q)
            E.dep("scalar", *qn_done)
            e_lnq = sop(lambda e: e.activation(A.ap("qm2", 0, 128, 0, NH),
                                               A.ap("qm2", 0, 128, 0, NH), AF.Ln))
            E.dep("vector", e_lnq)
            e_vadd = vop(lambda e, pb=pb: e.tensor_tensor(A.ap("qm2", 0, 128, 0, NH),
                                                          A.ap("qm2", 0, 128, 0, NH),
                                                          A.ap("lnmkb", 0, 128, pb * NH, pb * NH + NH),
                                                          AL.add))
            E.dep("scalar", e_vadd)
            e_exp = sop(lambda e: e.activation(A.ap("qm2", 0, 128, 0, NH), A.ap("qm2", 0, 128, 0, NH),
                                               AF.Exp, scale=0.5, bias=LN8TH))
            E.dep("vector", e_exp)
            e_negc = vop(lambda e: e.tensor_scalar(A.ap("negc", 0, 128, 0, NH),
                                                   A.ap("qm2", 0, 128, 0, NH), -1.0, None, AL.mult))

            accA = accB = None
            first_pa_dep = [pa_scal_dep, avg_scal_dep]
            first_pb_dep = list(pb_scal_dep)
            pb_scal_dep = []
            for p in range(NH // 2):
                hA, hB = 2 * p, 2 * p + 1
                exp_last = {}
                for ci, (off, w) in enumerate(chunks):
                    for (hh, bi, prow) in ((hA, 0 + (ci % 2), 0), (hB, 2 + (ci % 2), 64)):
                        E.dep("tensor", qt_ev[p], e_negc)
                        bdep(bi)
                        e_mm = mm(psum[bi][:, 0:w],
                                  QTg_ap(p, ti * 128, ti * 128 + 128, prow, prow + 64),
                                  KTap(p, off, off + w, prow, prow + 64))
                        qtg_cons.append(e_mm)
                        mask_ev = None
                        if off + w == ext:
                            E.dep("vector", e_mm)
                            mask_ev = vop(lambda e, bi=bi, o=dcol - off: e.tensor_tensor(
                                psum[bi][:, o:o + 128], psum[bi][:, o:o + 128], TRI, AL.add))
                        Pbuf = "PA" if hh == hA else "PB"
                        scol = 0 if hh == hA else 8
                        E.dep("scalar", e_mm, mask_ev)
                        if p == 0 and ci == 0:
                            E.dep("scalar", *(first_pa_dep if Pbuf == "PA" else first_pb_dep))
                        E.dep("scalar", accA if hh == hA else accB)
                        e_x = sop(lambda e, bi=bi, off=off, w=w, hh=hh, Pbuf=Pbuf, scol=scol, ci=ci:
                                  e.activation(A.ap(Pbuf, 0, 128, off, off + w), psum[bi][:, 0:w],
                                               AF.Exp, scale=0.125,
                                               bias=A.ap("negc", 0, 128, hh, hh + 1),
                                               accum_out=A.ap("sch", 0, 128, scol + ci, scol + ci + 1)))
                        BB[bi] = e_x
                        exp_last[hh] = e_x
                e_s16 = None
                for (col, hh, scol) in ((10, hA, 0), (11, hB, 8)):
                    E.dep("vector", exp_last[hh])
                    vop(lambda e, col=col, scol=scol, n=len(chunks): e.reduce_sum(
                        A.ap("small", 0, 128, col, col + 1), A.ap("sch", 0, 128, scol, scol + n),
                        axis=AX.X))
                    e_s16 = vop(lambda e, col=col: e.tensor_scalar(
                        A.ap("small", 0, 128, col, col + 1), A.ap("small", 0, 128, col, col + 1),
                        16.0, 1e-37, AL.mult, AL.max))
                E.dep("scalar", e_s16)
                sop(lambda e: e.activation(A.ap("small", 0, 128, 12, 14),
                                           A.ap("small", 0, 128, 10, 12), AF.Ln))
                e_rexp = sop(lambda e: e.activation(A.ap("small", 0, 128, 12, 14),
                                                    A.ap("small", 0, 128, 12, 14), AF.Exp, scale=-1.0))
                for (col, hh, Pbuf) in ((12, hA, "PA"), (13, hB, "PB")):
                    if p == 0 and hh == hA:
                        E.dep("vector", e_rexp, *uld_evs)
                    else:
                        E.dep("vector", e_rexp)
                    e_acc = vop(lambda e, Pbuf=Pbuf, ext=ext, col=col: e.scalar_tensor_tensor(
                        A.ap("avg", 0, 128, 0, ext), A.ap(Pbuf, 0, 128, 0, ext),
                        A.ap("small", 0, 128, col, col + 1), A.ap("avg", 0, 128, 0, ext),
                        AL.mult, AL.add))
                    if hh == hA:
                        accA = e_acc
                    else:
                        accB = e_acc

            E.dep("vector", accA, accB)
            e_msk = vop(lambda e, dcol=dcol: e.tensor_tensor(
                A.ap("avg", 0, 128, dcol, dcol + 128),
                A.ap("avg", 0, 128, dcol, dcol + 128), TRI, AL.add))
            E.dep("scalar", e_msk, pa_scal_dep)
            e_p2 = sop(lambda e, ext=ext: e.activation(A.ap("PA", 0, 128, 0, ext),
                                                       A.ap("avg", 0, 128, 0, ext), AF.Exp, scale=0.5))
            prev_p2 = e_p2

            hid_busy = [None, None]
            e_mm3 = None
            for kc in range(gi + 1):
                par = kc % 2
                E.dep("tensor", e_p2)
                bdep(4 + par)
                e_tr = tr(psum[4 + par][:, 0:128], A.ap("PA", 0, 128, kc * 128, kc * 128 + 128))
                pa_scal_dep = e_tr
                E.dep("vector", e_tr)
                e_dw = vop(lambda e, par=par: e.tensor_copy(
                    A.ap("dwT", 0, 128, par * 128, par * 128 + 128), psum[4 + par][:, 0:128]))
                BB[4 + par] = e_dw
                if kc < 2:
                    E.dep("sync", accB)
                E.dep("sync", hid_busy[par])
                e_h = ld(hid_ap(par, 0, D), x0[kc * 128:(kc + 1) * 128, :],
                         stream=f"L{par}")
                E.dep("vector", e_h)
                e_o1 = vop(lambda e, par=par: e.memset(hid_ap(par, D, D + 1), 1.0))
                E.dep("tensor", e_dw, e_o1)
                if kc == 0:
                    E.dep("tensor", x1t_busy)
                    bdep(*range(DH + 1))
                for s in range(DH):
                    mm(psum[s][:, 0:DW], A.ap("dwT", 0, 128, par * 128, par * 128 + 128),
                       hid_ap(par, s * DW, s * DW + DW), start=(kc == 0), stop=(kc == gi))
                e_mm3 = mm(psum[DH][0:128, 0:1], A.ap("dwT", 0, 128, par * 128, par * 128 + 128),
                           hid_ap(par, D, D + 1), start=(kc == 0), stop=(kc == gi))
                hid_busy[par] = e_mm3
                pb_scal_dep.append(e_mm3)

            E.dep("sync", accB, *first_pb_dep)
            e_x0 = ld(x0t_ap(), x0[gi * 128:(gi + 1) * 128, :], stream="LX")
            E.dep("vector", e_mm3)
            e_cl = vop(lambda e: e.tensor_scalar(A.ap("small", 0, 128, 8, 9), psum[DH][0:128, 0:1],
                                                 1e-6, None, AL.max))
            BB[DH] = e_cl
            vop(lambda e: e.reciprocal(A.ap("small", 0, 128, 9, 10), A.ap("small", 0, 128, 8, 9)))
            E.dep("vector", e_x0, x1t_busy)
            ep_evs = []
            for s in range(DH):
                e_p = vop(lambda e, s=s: e.scalar_tensor_tensor(
                    A.ap("x1t", 0, 128, s * DW, s * DW + DW), psum[s][:, 0:DW],
                    A.ap("small", 0, 128, 9, 10), x0t_ap(s * DW, s * DW + DW),
                    AL.mult, AL.add))
                BB[s] = e_p
                ep_evs.append(e_p)
            pb_scal_dep.extend(ep_evs)
            e_ln, e_lnsq = layer_norm(A.ap("x1t"), A.ap("x1t"), A.ap("avg", 0, 128, 0, D), deps=ep_evs)
            avg_scal_dep = e_lnsq
            E.dep("gpsimd", e_ln, e_lnsq)
            x1t_busy = st(x1_hbm[gi * 128:(gi + 1) * 128, :], A.ap("x1t"))
            x1_store_last = x1t_busy

    A.free("avg", "PA", "PB", "QTg", "x0Tq", "wslab", "sch", "negc", "qm2",
           "ush", "tsh", "dwT", "x1t", "KT", "lnmkb")
    E.barrier()
    BB.clear()

    QSEG = OWN // 2
    NSEG = 2
    NQBS = QSEG // QB
    assert NQBS >= 1 and QSEG % QB == 0
    A.alloc("KT2", HT * T)
    A.alloc("V2", TT * (HPR * 65))
    A.alloc("QT2", HT * QSEG)
    A.alloc("wbuf", DC * 512)
    A.alloc("hnT", DC * 128)
    A.alloc("hnt", D)
    A.alloc("x1b", D)
    A.alloc("PT", 4 * 512)
    A.alloc("att", 512)
    A.alloc("Rr", 512)
    A.alloc("Rb", 512)
    A.alloc("b2c", 2 * DC)
    A.alloc("qmk2", 40)
    A.alloc("negc2", HPR)

    def KT2ap(j, c0, c1, p0=0, p1=128):
        return A.ap("KT2", p0, p1, j * T + c0, j * T + c1)

    def V2ap(kt, h, c0, c1):
        base = kt * (HPR * 65) + h * 65
        return A.ap("V2", 0, 128, base + c0, base + c1)

    def V2full(kt):
        return A.ap("V2", 0, 128, kt * (HPR * 65), (kt + 1) * (HPR * 65))

    def QT2ap(j, c0, c1, p0=0, p1=128):
        return A.ap("QT2", p0, p1, j * QSEG + c0, j * QSEG + c1)

    A.alloc("lnb1", 2 * D)
    A.alloc("bvbs", D)
    ld(A.ap("b2c", 0, 128, 0, DC), bq2.rearrange("(c p) -> p c", p=128), slow=True, stream="LB")
    ld(A.ap("b2c", 0, 128, DC, 2 * DC), bk2.rearrange("(c p) -> p c", p=128), slow=True, stream="LB")
    ld(A.ap("lnb1", 0, 128, 0, D), g1b[:, :], stream="LB")
    ld(A.ap("lnb1", 0, 128, D, 2 * D), b1lnb[:, :], stream="LB")
    e_ph3c = ld(A.ap("bvbs", 0, 128, 0, D), bv2b[:, :], stream="LB")
    for _eng in ("tensor", "vector", "scalar", "gpsimd"):
        E.dep(_eng, e_ph3c)
    G1AP = A.ap("lnb1", 0, 128, 0, D)
    B1AP = A.ap("lnb1", 0, 128, D, 2 * D)

    atto_store_last = None
    att_busy = None
    for r in range(2):
        hbase = r * HPR
        fbase = r * HW

        wb_cons = []
        hn_last = (None, None)
        hnT_cons = None
        for which in ("K", "V"):
            E.dep("sync", *wb_cons)
            wb_cons = []
            wsrc = {"K": wk2, "V": wv2}[which]
            ev_wb = [ld(A.ap("wbuf", 0, 128, fi * 512, fi * 512 + HW),
                        wsrc[fi * 128:(fi + 1) * 128, fbase:fbase + HW], stream="LW")
                     for fi in range(DC)]
            for kt in range(TT):
                E.dep("sync", hn_last[0], hn_last[1], x1_store_last)
                e_x1 = ld(A.ap("x1b"), x1_hbm[kt * 128:(kt + 1) * 128, :])
                junk = A.ap("QT2", 0, 128, 0, D)
                e_hn, e_hnsq = layer_norm(A.ap("x1b"), A.ap("hnt"), junk,
                                          g_ap=G1AP, b_ap=B1AP, deps=[e_x1])
                hn_last = (e_hn, e_hnsq)
                trev = {}
                for fc in range(DC):
                    E.dep("tensor", e_hn)
                    bdep(4 + fc % 2)
                    e_tp = tr(psum[4 + fc % 2][:, 0:128],
                              A.ap("hnt", 0, 128, fc * 128, fc * 128 + 128))
                    E.dep("vector", e_tp, hnT_cons)
                    e_hT = vop(lambda e, fc=fc: e.tensor_copy(
                        A.ap("hnT", 0, 128, fc * 128, fc * 128 + 128), psum[4 + fc % 2][:, 0:128]))
                    BB[4 + fc % 2] = e_hT
                    trev[fc] = e_hT
                if which == "K":
                    for j in range(HT):
                        bi = j % 2
                        bdep(bi)
                        e_mmx = None
                        for fi in range(DC):
                            E.dep("tensor", trev[fi], ev_wb[fi])
                            e_mmx = mm(psum[bi][:, 0:128],
                                       A.ap("wbuf", 0, 128, fi * 512 + j * 128, fi * 512 + j * 128 + 128),
                                       A.ap("hnT", 0, 128, fi * 128, fi * 128 + 128),
                                       start=(fi == 0), stop=(fi == DC - 1))
                        wb_cons.append(e_mmx)
                        hnT_cons = e_mmx
                        E.dep("scalar", e_mmx)
                        e_evx = sop(lambda e, bi=bi, j=j, kt=kt, fbase=fbase: e.activation(
                            KT2ap(j, kt * 128, kt * 128 + 128), psum[bi][:, 0:128], AF.Identity,
                            bias=A.ap("b2c", 0, 128, DC + (fbase // 128) + j,
                                      DC + (fbase // 128) + j + 1)))
                        BB[bi] = e_evx
                else:
                    bdep(2)
                    e_mmv = None
                    for fi in range(DC):
                        E.dep("tensor", trev[fi], ev_wb[fi])
                        e_mmv = mm(psum[2][:, 0:HW],
                                   A.ap("hnT", 0, 128, fi * 128, fi * 128 + 128),
                                   A.ap("wbuf", 0, 128, fi * 512, fi * 512 + HW),
                                   start=(fi == 0), stop=(fi == DC - 1))
                    wb_cons.append(e_mmv)
                    hnT_cons = e_mmv
                    E.dep("vector", e_mmv)
                    e_vv = vop(lambda e, kt=kt, fbase=fbase: e.tensor_tensor(
                        V2full(kt).rearrange("p (h x) -> p h x", x=65)[:, :, 0:64],
                        psum[2][:, 0:HW].rearrange("p (h x) -> p h x", x=64),
                        A.ap("bvbs", 0, 128, fbase, fbase + HW).rearrange("p (h x) -> p h x", x=64),
                        AL.add))
                    BB[2] = e_vv
                    vop(lambda e, kt=kt: e.memset(
                        V2full(kt).rearrange("p (h x) -> p h x", x=65)[:, :, 64:65], 1.0))

        E.barrier()
        BB.clear()
        qm_done = []
        sq_cons = None
        for j in range(HT):
            run_ev = None
            for cb in range(T // 512):
                E.dep("scalar", sq_cons)
                e_sq = sop(lambda e, j=j, cb=cb: e.activation(
                    A.ap("PT", 0, 128, 0, 512), KT2ap(j, cb * 512, cb * 512 + 512), AF.Square))
                E.dep("tensor", e_sq)
                bdep(2)
                e_n = mm(psum[2][0:2, 0:512], A.ap("e2", 0, 128, 0, 2), A.ap("PT", 0, 128, 0, 512))
                sq_cons = e_n
                E.dep("vector", e_n)
                e_r = vop(lambda e: e.reduce_max(A.ap("small", 0, 2, 0, 1),
                                                 psum[2][0:2, 0:512], axis=AX.X))
                BB[2] = e_r
                if cb == 0:
                    run_ev = vop(lambda e: e.tensor_copy(A.ap("small", 0, 2, 2, 3),
                                                         A.ap("small", 0, 2, 0, 1)))
                else:
                    run_ev = vop(lambda e: e.tensor_tensor(A.ap("small", 0, 2, 2, 3),
                                                           A.ap("small", 0, 2, 2, 3),
                                                           A.ap("small", 0, 2, 0, 1), AL.max))
            E.dep("tensor", run_ev)
            bdep(3)
            e_t = tr(psum[3][0:1, 0:2], A.ap("small", 0, 2, 2, 3))
            E.dep("vector", e_t)
            e_qm = vop(lambda e, j=j: e.tensor_copy(
                A.ap("qmk2", 0, 1, 16 + 2 * j, 16 + 2 * j + 2), psum[3][0:1, 0:2]))
            BB[3] = e_qm
            qm_done.append(e_qm)
        E.dep("scalar", *qm_done)
        e_lnK = sop(lambda e: e.activation(A.ap("qmk2", 0, 1, 16, 16 + HPR),
                                           A.ap("qmk2", 0, 1, 16, 16 + HPR), AF.Ln))

        E.dep("sync", *wb_cons)
        wb_cons = []
        ev_wb = [ld(A.ap("wbuf", 0, 128, fi * 512, fi * 512 + HW),
                    wq2[fi * 128:(fi + 1) * 128, fbase:fbase + HW], stream="LW")
                 for fi in range(DC)]

        for seg in range(NSEG):
            sbase = seg * (QSEG // 128)
            for ot in range(QSEG // 128):
                E.dep("gpsimd", hn_last[0], hn_last[1])
                e_x1 = ld_x1_own(A.ap("x1b"), sbase + ot)
                e_hn, e_hnsq = layer_norm(A.ap("x1b"), A.ap("hnt"), A.ap("PT", 0, 128, 0, D),
                                          g_ap=G1AP, b_ap=B1AP, deps=[e_x1])
                hn_last = (e_hn, e_hnsq)
                trev = {}
                for fc in range(DC):
                    E.dep("tensor", e_hn)
                    bdep(4 + fc % 2)
                    e_tp = tr(psum[4 + fc % 2][:, 0:128],
                              A.ap("hnt", 0, 128, fc * 128, fc * 128 + 128))
                    E.dep("vector", e_tp, hnT_cons)
                    e_hT = vop(lambda e, fc=fc: e.tensor_copy(
                        A.ap("hnT", 0, 128, fc * 128, fc * 128 + 128), psum[4 + fc % 2][:, 0:128]))
                    BB[4 + fc % 2] = e_hT
                    trev[fc] = e_hT
                for j in range(HT):
                    bi = j % 2
                    bdep(bi)
                    e_mmq = None
                    for fi in range(DC):
                        E.dep("tensor", trev[fi], ev_wb[fi])
                        e_mmq = mm(psum[bi][:, 0:128],
                                   A.ap("wbuf", 0, 128, fi * 512 + j * 128, fi * 512 + j * 128 + 128),
                                   A.ap("hnT", 0, 128, fi * 128, fi * 128 + 128),
                                   start=(fi == 0), stop=(fi == DC - 1))
                    wb_cons.append(e_mmq)
                    hnT_cons = e_mmq
                    E.dep("scalar", e_mmq)
                    e_evq = sop(lambda e, bi=bi, j=j, ot=ot, fbase=fbase: e.activation(
                        QT2ap(j, ot * 128, ot * 128 + 128), psum[bi][:, 0:128], AF.Identity,
                        bias=A.ap("b2c", 0, 128, (fbase // 128) + j, (fbase // 128) + j + 1)))
                    BB[bi] = e_evq

            qm_done = []
            for j in range(HT):
                run_ev = None
                ncb = (QSEG + 511) // 512
                for cb in range(ncb):
                    w = min(512, QSEG - cb * 512)
                    E.dep("scalar", sq_cons)
                    e_sq = sop(lambda e, j=j, cb=cb, w=w: e.activation(
                        A.ap("PT", 0, 128, 0, w), QT2ap(j, cb * 512, cb * 512 + w), AF.Square))
                    E.dep("tensor", e_sq)
                    bdep(2)
                    e_n = mm(psum[2][0:2, 0:w], A.ap("e2", 0, 128, 0, 2), A.ap("PT", 0, 128, 0, w))
                    sq_cons = e_n
                    E.dep("vector", e_n)
                    e_r = vop(lambda e, w=w: e.reduce_max(A.ap("small", 0, 2, 0, 1),
                                                          psum[2][0:2, 0:w], axis=AX.X))
                    BB[2] = e_r
                    if cb == 0:
                        run_ev = vop(lambda e: e.tensor_copy(A.ap("small", 0, 2, 2, 3),
                                                             A.ap("small", 0, 2, 0, 1)))
                    else:
                        run_ev = vop(lambda e: e.tensor_tensor(A.ap("small", 0, 2, 2, 3),
                                                               A.ap("small", 0, 2, 2, 3),
                                                               A.ap("small", 0, 2, 0, 1), AL.max))
                E.dep("tensor", run_ev)
                bdep(3)
                e_t = tr(psum[3][0:1, 0:2], A.ap("small", 0, 2, 2, 3))
                E.dep("vector", e_t)
                e_qm = vop(lambda e, j=j: e.tensor_copy(
                    A.ap("qmk2", 0, 1, 2 * j, 2 * j + 2), psum[3][0:1, 0:2]))
                BB[3] = e_qm
                qm_done.append(e_qm)
            E.dep("scalar", *qm_done, e_lnK)
            sop(lambda e: e.activation(A.ap("qmk2", 0, 1, 0, HPR), A.ap("qmk2", 0, 1, 0, HPR), AF.Ln))
            E.dep("vector", E.last_compute["scalar"])
            e_vadd2 = vop(lambda e: e.tensor_tensor(A.ap("qmk2", 0, 1, 0, HPR),
                                                    A.ap("qmk2", 0, 1, 0, HPR),
                                                    A.ap("qmk2", 0, 1, 16, 16 + HPR), AL.add))
            E.dep("scalar", e_vadd2)
            e_exp2 = sop(lambda e: e.activation(A.ap("qmk2", 0, 1, 0, HPR), A.ap("qmk2", 0, 1, 0, HPR),
                                                AF.Exp, scale=0.5, bias=LN8TH))
            E.dep("tensor", e_exp2)
            bdep(3)
            e_bc2 = mm(psum[3][:, 0:HPR], ONES[0:1, :], A.ap("qmk2", 0, 1, 0, HPR))
            E.dep("vector", e_bc2)
            e_negc2 = vop(lambda e: e.tensor_scalar(A.ap("negc2", 0, 128, 0, HPR), psum[3][:, 0:HPR],
                                                    -1.0, None, AL.mult))
            BB[3] = e_negc2

            pt_busy = {}
            rr_busy = None
            for qbl in range(NQBS):
                qb = seg * NQBS + qbl
                for jp in range(HT):
                    h0, h1 = 2 * jp, 2 * jp + 1
                    av_done = {}
                    for kc in range(KC):
                        e_exp = {}
                        for (m, hh, prow) in ((0, h0, 0), (1, h1, 64)):
                            bi = (2 * m) + (kc % 2)
                            E.dep("tensor", e_negc2)
                            bdep(bi)
                            e_s = mm(psum[bi][:, 0:QB],
                                     KT2ap(jp, kc * 128, kc * 128 + 128, prow, prow + 64),
                                     QT2ap(jp, qbl * QB, qbl * QB + QB, prow, prow + 64))
                            E.dep("scalar", e_s, pt_busy.get((m, kc % 2)))
                            e_e = sop(lambda e, bi=bi, m=m, kc=kc, hh=hh: e.activation(
                                A.ap("PT", 0, 128, (2 * m + kc % 2) * 512, (2 * m + kc % 2) * 512 + QB),
                                psum[bi][:, 0:QB], AF.Exp, scale=0.125,
                                bias=A.ap("negc2", 0, 128, hh, hh + 1)))
                            BB[bi] = e_e
                            e_exp[m] = e_e
                        for (m, hh) in ((0, h0), (1, h1)):
                            E.dep("tensor", e_exp[m])
                            if kc == 0:
                                bdep(4 + m)
                            e_av = mm(psum[4 + m][0:65, 0:QB],
                                      V2ap(kc, hh, 0, 65),
                                      A.ap("PT", 0, 128, (2 * m + kc % 2) * 512,
                                           (2 * m + kc % 2) * 512 + QB),
                                      start=(kc == 0), stop=(kc == KC - 1))
                            pt_busy[(m, kc % 2)] = e_av
                            av_done[m] = e_av
                    for (m, hh) in ((0, h0), (1, h1)):
                        E.dep("vector", av_done[m], rr_busy)
                        e_sr = vop(lambda e, m=m: e.tensor_copy(A.ap("Rr", 64, 65, 0, QB),
                                                                psum[4 + m][64:65, 0:QB]))
                        e_rc = vop(lambda e: e.reciprocal(A.ap("Rr", 64, 65, 0, QB),
                                                          A.ap("Rr", 64, 65, 0, QB)))
                        E.dep("tensor", e_rc)
                        bdep(6)
                        e_Rb = mm(psum[6][0:64, 0:QB], ONES[64:65, 0:64], A.ap("Rr", 64, 65, 0, QB))
                        rr_busy = e_Rb
                        E.dep("vector", e_Rb)
                        e_Rs = vop(lambda e: e.tensor_copy(A.ap("Rb", 0, 64, 0, QB), psum[6][0:64, 0:QB]))
                        BB[6] = e_Rs
                        E.dep("vector", att_busy)
                        e_at = vop(lambda e, m=m: e.tensor_tensor(A.ap("att", 0, 64, 0, QB),
                                                                  psum[4 + m][0:64, 0:QB],
                                                                  A.ap("Rb", 0, 64, 0, QB), AL.mult))
                        BB[4 + m] = e_at
                        E.dep("gpsimd", e_at)
                        e_st = st(attoT_hbm[fbase + hh * 64:fbase + hh * 64 + 64,
                                            qb * QB:qb * QB + QB],
                                  A.ap("att", 0, 64, 0, QB))
                        att_busy = e_st
                        atto_store_last = e_st
        E.barrier()
        BB.clear()

    A.free("KT2", "V2", "QT2", "wbuf", "hnT", "hnt", "x1b", "PT", "att", "Rr",
           "Rb", "b2c", "qmk2", "negc2", "lnb1", "bvbs")

    A.alloc("wo_r", DC * D)
    A.alloc("aTl", 256)
    A.alloc("x1b4", D)
    A.alloc("x2t", D)
    A.alloc("hn2t", D)
    A.alloc("junk4", D)
    A.alloc("gwb", DC * NE)
    A.alloc("gbbs", NE)
    A.alloc("onehotT", OWN)
    LGB = (NE + 7) & ~7
    A.alloc("lgb", 4 * LGB)
    A.alloc("hnTg", 256)

    A.alloc("lnb2", 2 * D)
    A.alloc("bobs", D)
    ld(A.ap("lnb2", 0, 128, 0, D), g2b[:, :], stream="LB")
    ld(A.ap("lnb2", 0, 128, D, 2 * D), b2lnb[:, :], stream="LB")
    e_ph4c = ld(A.ap("bobs", 0, 128, 0, D), bo2b[:, :], stream="LB")
    for _eng in ("tensor", "vector", "scalar", "gpsimd"):
        E.dep(_eng, e_ph4c)
    G2AP = A.ap("lnb2", 0, 128, 0, D)
    B2AP = A.ap("lnb2", 0, 128, D, 2 * D)
    ev_wo = [ld(A.ap("wo_r", 0, 128, fc * D, fc * D + D), wo2[fc * 128:(fc + 1) * 128, :],
                 stream="LW") for fc in range(DC)]
    ld(A.ap("gwb", 0, 128, 0, DC * NE).rearrange("p (c n) -> p c n", n=NE),
       gwD.rearrange("(c p) n -> p c n", p=128), stream="LW")
    ld(A.ap("gbbs", 0, 128, 0, NE), gbb[:, :], stream="LW")

    hn2_store_last = None
    x2_store_last = None
    x2t_st_prev = None
    hn2t_st_prev = None
    oh_done = []
    aTl_cons = None
    ep_prev = []
    hn2sq_prev = None
    tp_last_prev = None
    lg_last_prev = None
    tp2_prev = None
    for i in range(OT):
        e_mo = {}
        for fc in range(DC):
            E.dep("sync", aTl_cons)
            if fc == 0 and i == 0:
                E.dep("sync", atto_store_last)
            e_a = ld(A.ap("aTl", 0, 128, (fc % 2) * 128, (fc % 2) * 128 + 128),
                     attoT_hbm[fc * 128:(fc + 1) * 128, i * 128:(i + 1) * 128],
                     stream=f"L{fc % 2}")
            for s in range(DH):
                E.dep("tensor", e_a, ev_wo[fc])
                if fc == 0:
                    bdep(s)
                e_mo[s] = mm(psum[s][:, 0:DW],
                             A.ap("aTl", 0, 128, (fc % 2) * 128, (fc % 2) * 128 + 128),
                             A.ap("wo_r", 0, 128, fc * D + s * DW, fc * D + s * DW + DW),
                             start=(fc == 0), stop=(fc == DC - 1))
            aTl_cons = e_mo[DH - 1]
        E.dep("gpsimd", *ep_prev)
        e_x1 = ld_x1_own(A.ap("x1b4"), i)
        ep = []
        for s in range(DH):
            E.dep("vector", e_mo[s], e_x1, x2t_st_prev, hn2sq_prev)
            e_p = vop(lambda e, s=s: e.scalar_tensor_tensor(
                A.ap("x2t", 0, 128, s * DW, s * DW + DW), psum[s][:, 0:DW], 1.0,
                A.ap("x1b4", 0, 128, s * DW, s * DW + DW), AL.mult, AL.add))
            BB[s] = e_p
            ep.append(e_p)
        e_bo = vop(lambda e: e.tensor_tensor(A.ap("x2t"), A.ap("x2t"), A.ap("bobs"), AL.add))
        E.dep("gpsimd", e_bo)
        e_sx2 = st(x2_hbm[i * 128:(i + 1) * 128, :], A.ap("x2t"))
        x2_store_last = e_sx2
        _lndeps = [e_bo]
        if hn2t_st_prev is not None:
            _lndeps.append(hn2t_st_prev)
        if tp_last_prev is not None:
            _lndeps.append(tp_last_prev)
        e_hn2, e_hn2sq = layer_norm(A.ap("x2t"), A.ap("hn2t"), A.ap("junk4"),
                                    g_ap=G2AP, b_ap=B2AP, deps=_lndeps)
        x2t_st_prev = e_sx2
        hn2sq_prev = e_hn2sq
        ep_prev = ep
        E.dep("gpsimd", e_hn2)
        e_sh2 = st(hn2_hbm[i * 128:(i + 1) * 128, :], A.ap("hn2t"))
        hn2_store_last = e_sh2
        hn2t_st_prev = e_sh2
        e_lg = None
        for fc in range(DC):
            E.dep("tensor", e_hn2)
            bdep(6)
            e_tp = tr(psum[6][:, 0:128], A.ap("hn2t", 0, 128, fc * 128, fc * 128 + 128))
            tp_last = e_tp
            E.dep("vector", e_tp, lg_last_prev if fc < 2 else None)
            e_hT = vop(lambda e, fc=fc: e.tensor_copy(
                A.ap("hnTg", 0, 128, (fc % 2) * 128, (fc % 2) * 128 + 128), psum[6][:, 0:128]))
            BB[6] = e_hT
            E.dep("tensor", e_hT)
            if fc == 0:
                bdep(7)
            e_lg = mm(psum[7][0:128, 0:NE],
                      A.ap("hnTg", 0, 128, (fc % 2) * 128, (fc % 2) * 128 + 128),
                      A.ap("gwb", 0, 128, fc * NE, fc * NE + NE),
                      start=(fc == 0), stop=(fc == DC - 1))
        E.dep("vector", e_lg, tp2_prev)
        e_v1 = vop(lambda e: e.tensor_tensor(A.ap("lgb", 0, 128, 0, NE), psum[7][0:128, 0:NE],
                                             A.ap("gbbs", 0, 128, 0, NE), AL.add))
        BB[7] = e_v1
        vop(lambda e: e.reduce_max(A.ap("small", 0, 128, 8, 9), A.ap("lgb", 0, 128, 0, NE), axis=AX.X))
        vop(lambda e: e.tensor_scalar(A.ap("lgb", 0, 128, LGB, LGB + NE),
                                      A.ap("lgb", 0, 128, 0, NE),
                                      A.ap("small", 0, 128, 8, 9), None, AL.is_ge))
        vop(lambda e: e.memset(A.ap("lgb", 0, 128, 2 * LGB, 2 * LGB + NE), 1.0))
        vop(lambda e: e.tensor_tensor_scan(A.ap("lgb", 0, 128, 3 * LGB, 3 * LGB + NE),
                                           A.ap("lgb", 0, 128, 2 * LGB, 2 * LGB + NE),
                                           A.ap("lgb", 0, 128, LGB, LGB + NE),
                                           0.0, AL.mult, AL.add))
        vop(lambda e: e.tensor_scalar(A.ap("lgb", 0, 128, 3 * LGB, 3 * LGB + NE),
                                      A.ap("lgb", 0, 128, 3 * LGB, 3 * LGB + NE),
                                      1.0, None, AL.is_equal))
        e_oh = vop(lambda e: e.tensor_tensor(A.ap("lgb", 0, 128, LGB, LGB + NE),
                                             A.ap("lgb", 0, 128, LGB, LGB + NE),
                                             A.ap("lgb", 0, 128, 3 * LGB, 3 * LGB + NE), AL.mult))
        E.dep("tensor", e_oh)
        bdep(6)
        e_tp2 = tr(psum[6][0:NE, 0:128], A.ap("lgb", 0, 128, LGB, LGB + NE))
        E.dep("vector", e_tp2)
        e_ohT = vop(lambda e, i=i: e.tensor_copy(
            A.ap("onehotT", 0, NE, i * 128, (i + 1) * 128), psum[6][0:NE, 0:128]))
        BB[6] = e_ohT
        oh_done.append(e_ohT)
        tp_last_prev = tp_last
        lg_last_prev = e_lg
        tp2_prev = e_tp2
    E.barrier()

    A.free("wo_r", "aTl", "x1b4", "x2t", "hn2t", "junk4", "gwb", "gbbs", "lgb", "hnTg",
           "lnb2", "bobs")
    BB.clear()

    A.alloc("ones8", OWN)
    A.alloc("csT", OWN)
    A.alloc("destT", OWN)
    A.alloc("destc", (OT + 7) & ~7)
    AU.alloc("desti", (OT + 7) & ~7)
    AU.alloc("iotat", (OT + 7) & ~7)
    AU.alloc("invt", (SC + 7) & ~7)
    AU.alloc("uz", (SC + 7) & ~7)

    E.dep("vector", *oh_done)
    vop(lambda e: e.memset(A.ap("ones8", 0, NE, 0, OWN), 1.0))
    vop(lambda e: e.tensor_tensor_scan(A.ap("csT", 0, NE, 0, OWN), A.ap("ones8", 0, NE, 0, OWN),
                                       A.ap("onehotT", 0, NE, 0, OWN), 0.0, AL.mult, AL.add))
    vop(lambda e: e.tensor_tensor(A.ap("csT", 0, NE, 0, OWN), A.ap("csT", 0, NE, 0, OWN),
                                  A.ap("onehotT", 0, NE, 0, OWN), AL.subtract))
    e_dp = vop(lambda e: e.scalar_tensor_tensor(A.ap("csT", 0, NE, 0, OWN),
                                                A.ap("csT", 0, NE, 0, OWN),
                                                A.ap("basecol", 0, NE, 0, 1),
                                                A.ap("onehotT", 0, NE, 0, OWN), AL.add, AL.mult))
    dt_prev = None
    dt_done = []
    for cb in range(OWN // 512):
        E.dep("tensor", e_dp, dt_prev)
        e_d = mm(psum[0][0:1, 0:512], ONES[0:NE, 0:1], A.ap("csT", 0, NE, cb * 512, cb * 512 + 512))
        E.dep("vector", e_d)
        dt_prev = vop(lambda e, cb=cb: e.tensor_copy(
            A.ap("destT", 0, 1, cb * 512, cb * 512 + 512), psum[0][0:1, 0:512]))
        dt_done.append(dt_prev)
    if debug:
        E.dep("gpsimd", *dt_done)
        st(dbg_dest[:, :], A.ap("destT", 0, 1, 0, OWN))
    dc_prev = None
    dc_done = []
    for i in range(OT):
        E.dep("tensor", *dt_done)
        E.dep("tensor", dc_prev)
        e_t = tr(psum[1][0:128, 0:1], A.ap("destT", 0, 1, i * 128, i * 128 + 128))
        E.dep("vector", e_t)
        dc_prev = vop(lambda e, i=i: e.tensor_copy(A.ap("destc", 0, 128, i, i + 1),
                                                   psum[1][0:128, 0:1]))
        dc_done.append(dc_prev)
    E.dep("vector", *dc_done)
    e_cast = vop(lambda e: e.tensor_copy(AU.ap("desti", 0, 128, 0, OT), A.ap("destc", 0, 128, 0, OT)))
    e_iot = gop(lambda e: e.iota(AU.ap("iotat", 0, 128, 0, OT), [[128, OT]], base=0,
                                 channel_multiplier=1, allow_small_or_imprecise_dtypes=True))
    e_uz = gop(lambda e: e.memset(AU.ap("uz", 0, 128, 0, SC), 0))
    E.force_wait("gpsimd", e_uz)
    e_zf = st(invtok_hbm.rearrange("(c p) o -> p (c o)", p=128), AU.ap("uz", 0, 128, 0, SC), slow=True)
    sc_evs = []
    for i in range(OT):
        E.dep("gpsimd", e_cast)
        E.force_wait("gpsimd", e_zf, e_iot)
        sc_evs.append(gop(lambda e, i=i: e.indirect_dma_start(
            out=invtok_hbm[:, :], out_offset=IOA(AU.ap("desti", 0, 128, i, i + 1), 0),
            in_=AU.ap("iotat", 0, 128, i, i + 1), in_offset=None), amt=16))
    E.dep("sync", *sc_evs)
    e_ivt = ld(AU.ap("invt", 0, 128, 0, SC), invtok_hbm.rearrange("(c p) o -> p (c o)", p=128), slow=True)
    A.free("ones8", "csT", "destT", "destc")
    E.barrier()
    BB.clear()

    A.alloc("ghT", DC * CAP)
    A.alloc("Gt", CS * D)
    A.alloc("w1b", 2 * DC * 128)
    A.alloc("w2b", 2 * D)
    A.alloc("midb", 2 * CAP)
    A.alloc("b1c", FC)
    A.alloc("b2r", D)
    A.alloc("b2bc", D)
    A.alloc("yev", 2 * DW)

    def ghTap(fc, c0, c1):
        return A.ap("ghT", 0, 128, fc * CAP + c0, fc * CAP + c1)

    NYB = CS * DH
    assert NYB <= 6, "y accumulator banks exceed budget"
    y_store_last = None
    yev_busy = [None, None]
    for ex in range(NE):
        g_evs = []
        for cs_ in range(CS):
            E.dep("gpsimd", e_ivt)
            if ex == 0 and cs_ == 0:
                E.force_wait("gpsimd", hn2_store_last)
            g_evs.append(gop(lambda e, cs_=cs_, ex=ex: e.indirect_dma_start(
                out=A.ap("Gt", 0, 128, cs_ * D, cs_ * D + D), out_offset=None,
                in_=hn2_hbm[:, :],
                in_offset=IOA(AU.ap("invt", 0, 128, ex * CS + cs_, ex * CS + cs_ + 1), 0)), amt=16))
        trn_last = None
        for cs_ in range(CS):
            for fc in range(DC):
                par = (cs_ * DC + fc) % 2
                E.dep("tensor", g_evs[cs_])
                bdep(6 + par)
                e_tp = tr(psum[6 + par][:, 0:128], A.ap("Gt", 0, 128, cs_ * D + fc * 128,
                                                        cs_ * D + fc * 128 + 128))
                E.dep("vector", e_tp)
                trn_last = vop(lambda e, cs_=cs_, fc=fc, par=par: e.tensor_copy(
                    ghTap(fc, cs_ * 128, cs_ * 128 + 128), psum[6 + par][:, 0:128]))
                BB[6 + par] = trn_last
        e_b1 = ld(A.ap("b1c", 0, 128, 0, FC), b1m[ex, :].rearrange("(c p) -> p c", p=128), slow=True)
        e_b2r = ld(A.ap("b2r", 0, 1, 0, D), b2m[ex:ex + 1, :])
        bb_evs = []
        for s in range(DH):
            E.dep("tensor", e_b2r)
            bdep(6)
            e_bb = mm(psum[6][:, 0:DW], ONES[0:1, :], A.ap("b2r", 0, 1, s * DW, s * DW + DW))
            E.dep("vector", e_bb)
            e_bbc = vop(lambda e, s=s: e.tensor_copy(
                A.ap("b2bc", 0, 128, s * DW, s * DW + DW), psum[6][:, 0:DW]))
            BB[6] = e_bbc
            bb_evs.append(e_bbc)
        w1_busy = [None, None]
        w2_busy = [None, None]
        e_y = {}
        for f2c in range(FC):
            par = f2c % 2
            E.dep("sync", w1_busy[par])
            e_w1 = ld(A.ap("w1b", 0, 128, par * DC * 128, (par + 1) * DC * 128)
                      .rearrange("p (hc f) -> p hc f", f=128),
                      w1[ex, :, f2c * 128:(f2c + 1) * 128].rearrange("(hc p) f -> p hc f", p=128),
                      stream=f"L{par}")
            E.dep("sync", w2_busy[par])
            e_w2 = ld(A.ap("w2b", 0, 128, par * D, (par + 1) * D),
                      w2[ex, f2c * 128:(f2c + 1) * 128, :], stream=f"L{par}")
            E.dep("tensor", e_w1, trn_last)
            bdep(6 + par)
            e_mid = None
            for hc in range(DC):
                e_mid = mm(psum[6 + par][:, 0:CAP],
                           A.ap("w1b", 0, 128, par * DC * 128 + hc * 128,
                                par * DC * 128 + hc * 128 + 128),
                           ghTap(hc, 0, CAP), start=(hc == 0), stop=(hc == DC - 1))
            w1_busy[par] = e_mid
            E.dep("scalar", e_mid, e_b1)
            e_rl = sop(lambda e, par=par, f2c=f2c: e.activation(
                A.ap("midb", 0, 128, par * CAP, par * CAP + CAP), psum[6 + par][:, 0:CAP],
                AF.Relu, bias=A.ap("b1c", 0, 128, f2c, f2c + 1)))
            BB[6 + par] = e_rl
            last_y = None
            for si in range(CS):
                for s in range(DH):
                    bi = si * DH + s
                    E.dep("tensor", e_rl, e_w2)
                    if f2c == 0:
                        E.dep("tensor", *bb_evs)
                        bdep(bi)
                    last_y = mm(psum[bi][:, 0:DW],
                                A.ap("midb", 0, 128, par * CAP + si * 128,
                                     par * CAP + si * 128 + 128),
                                A.ap("w2b", 0, 128, par * D + s * DW, par * D + s * DW + DW),
                                start=(f2c == 0), stop=(f2c == FC - 1))
                    e_y[bi] = last_y
            w2_busy[par] = last_y
        for si in range(CS):
            for s in range(DH):
                bi = si * DH + s
                par = bi % 2
                E.dep("vector", e_y[bi], yev_busy[par])
                e_ye = vop(lambda e, bi=bi, par=par, s=s: e.scalar_tensor_tensor(
                    A.ap("yev", 0, 128, par * DW, par * DW + DW), psum[bi][:, 0:DW], 1.0,
                    A.ap("b2bc", 0, 128, s * DW, s * DW + DW), AL.mult, AL.add))
                BB[bi] = e_ye
                E.dep("gpsimd", e_ye)
                e_sy = st(Y_hbm[ex * CAP + si * 128:ex * CAP + si * 128 + 128,
                                s * DW:s * DW + DW],
                          A.ap("yev", 0, 128, par * DW, par * DW + DW))
                yev_busy[par] = e_sy
                y_store_last = e_sy
        E.barrier()
        BB.clear()

    A.free("ghT", "Gt", "w1b", "w2b", "midb", "b1c", "b2r", "b2bc", "yev")

    A.alloc("mo", D)
    A.alloc("x1b7", D)
    A.alloc("x2b", D)
    A.alloc("tt", D)
    A.alloc("junk7", D)
    last_store = None
    for i in range(OT):
        if i == 0:
            E.force_wait("gpsimd", y_store_last)
        e_mo = gop(lambda e, i=i: e.indirect_dma_start(
            out=A.ap("mo"), out_offset=None, in_=Y_hbm[:, :],
            in_offset=IOA(AU.ap("desti", 0, 128, i, i + 1), 0)), amt=16)
        e_x1 = ld_x1_own(A.ap("x1b7"), i)
        if i == 0:
            E.dep("sync", x2_store_last)
        e_x2 = ld(A.ap("x2b"), x2_hbm[i * 128:(i + 1) * 128, :])
        E.dep("vector", e_x1, e_x2)
        e_t1 = vop(lambda e: e.tensor_tensor(A.ap("tt"), A.ap("x1b7"), A.ap("x2b"), AL.add))
        E.dep("vector", e_mo)
        e_t2 = vop(lambda e: e.tensor_tensor(A.ap("tt"), A.ap("tt"), A.ap("mo"), AL.add))
        e_ln, _ = layer_norm(A.ap("tt"), A.ap("tt"), A.ap("junk7"), deps=[e_t2])
        E.dep("gpsimd", e_ln)
        last_store = st(y_out[i * 128:(i + 1) * 128, :], A.ap("tt"))
        E.barrier()
    E.force_wait("gpsimd", last_store)

    E.assign_values()
    with nc.Block() as block:
        @block.sync
        def _(h):
            E.emit_engine("sync", h, sems)

        @block.tensor
        def _(h):
            E.emit_engine("tensor", h, sems)

        @block.vector
        def _(h):
            E.emit_engine("vector", h, sems)

        @block.scalar
        def _(h):
            E.emit_engine("scalar", h, sems)

        @block.gpsimd
        def _(h):
            E.emit_engine("gpsimd", h, sems)

    ctx.close()
    nc._dbgA, nc._dbgAU = A, AU
    return nc



K_CONSTS = CONST_VALUES


def host_constants(C):
    T, D, NE, CAP = C.T, C.D, C.NE, C.CAP
    ident = np.eye(128, dtype=np.float32)
    tri = np.zeros((128, 128), dtype=np.float32)
    for p in range(128):
        tri[p, p:] = NEGBIG
    ones = np.ones((128, 128), dtype=np.float32)
    e2 = np.zeros((128, 2), dtype=np.float32)
    e2[0:64, 0] = 1.0
    e2[64:128, 1] = 1.0
    basecol = (np.arange(NE, dtype=np.float32) * CAP).reshape(NE, 1)
    return ident, tri, ones, e2, basecol


def make_core_inputs(C, hs_b, u_b, ipw, ipb, ffn, half=0):
    T, D = C.T, C.D
    ident, tri, ones, e2, basecol = host_constants(C)
    bcast = lambda v: np.ascontiguousarray(np.broadcast_to(np.asarray(v, np.float32), (128, D)))
    bcN = lambda v: np.ascontiguousarray(np.broadcast_to(np.asarray(v, np.float32), (128, C.NE)))
    d = {
        "x0": np.ascontiguousarray(hs_b),
        "x0T": np.ascontiguousarray(hs_b.T),
        "u": np.ascontiguousarray(u_b),
        "wq1T": np.ascontiguousarray(ipw[0:D].T),
        "wk1T": np.ascontiguousarray(ipw[D:2 * D].T),
        "bq1": np.ascontiguousarray(ipb[0:D]),
        "bk1": np.ascontiguousarray(ipb[D:2 * D]),
        "wq2": np.ascontiguousarray(ffn["wq"]),
        "wk2": np.ascontiguousarray(ffn["wk"]),
        "wv2": np.ascontiguousarray(ffn["wv"]),
        "wo2": np.ascontiguousarray(ffn["wo"]),
        "bq2": np.ascontiguousarray(ffn["bq"]),
        "bk2": np.ascontiguousarray(ffn["bk"]),
        "bv2b": bcast(ffn["bv"]),
        "bo2b": bcast(ffn["bo"]),
        "g1b": bcast(ffn["ln1_g"]),
        "b1lnb": bcast(ffn["ln1_b"]),
        "g2b": bcast(ffn["ln2_g"]),
        "b2lnb": bcast(ffn["ln2_b"]),
        "gw": np.ascontiguousarray(ffn["gate_w"]),
        "gbb": bcN(ffn["gate_b"]),
        "w1": np.ascontiguousarray(ffn["w1"]),
        "b1m": np.ascontiguousarray(ffn["b1"]),
        "w2": np.ascontiguousarray(ffn["w2"]),
        "b2m": np.ascontiguousarray(ffn["b2"]),
        "cpool": np.ascontiguousarray(
            np.broadcast_to(np.array(list(K_CONSTS) + [0.0] * (16 - len(K_CONSTS)),
                                     np.float32), (128, 16))),
        "ident": ident,
        "trimask": tri,
        "ones": ones,
        "e2": e2,
        "basecol": basecol,
        "ownidx": (np.arange(C.T // 2, dtype=np.uint32) + half * (C.T // 2)).reshape(-1, 1),
    }
    return d


def kernel(
    L0_attn_hidden_states, L0_attn_in_proj_weight, L0_attn_in_proj_bias,
    L0_ffn_ln1_g, L0_ffn_ln1_b, L0_ffn_ln2_g, L0_ffn_ln2_b,
    L0_ffn_wq, L0_ffn_bq, L0_ffn_wk, L0_ffn_bk, L0_ffn_wv, L0_ffn_bv,
    L0_ffn_wo, L0_ffn_bo, L0_ffn_gate_w, L0_ffn_gate_b,
    L0_ffn_w1, L0_ffn_b1, L0_ffn_w2, L0_ffn_b2,
):
    C = FULL
    T, D = C.T, C.D
    hs = np.asarray(L0_attn_hidden_states, dtype=np.float32)
    B = hs.shape[0]

    import jax
    with jax.default_device(jax.devices("cpu")[0]):
        _u = jax.random.uniform(
            jax.random.key(42, impl="threefry2x32"), (B, T, T),
            jax.numpy.float32, 1e-06, 1.0 - 1e-06)
        u_all = np.asarray(-jax.numpy.log(-jax.numpy.log(_u)))

    ffn = {
        "wq": L0_ffn_wq, "wk": L0_ffn_wk, "wv": L0_ffn_wv, "wo": L0_ffn_wo,
        "bq": L0_ffn_bq, "bk": L0_ffn_bk, "bv": L0_ffn_bv, "bo": L0_ffn_bo,
        "ln1_g": L0_ffn_ln1_g, "ln1_b": L0_ffn_ln1_b,
        "ln2_g": L0_ffn_ln2_g, "ln2_b": L0_ffn_ln2_b,
        "gate_w": L0_ffn_gate_w, "gate_b": L0_ffn_gate_b,
        "w1": L0_ffn_w1, "b1": L0_ffn_b1, "w2": L0_ffn_w2, "b2": L0_ffn_b2,
    }
    ffn = {k: np.asarray(v, dtype=np.float32) for k, v in ffn.items()}
    ipw = np.asarray(L0_attn_in_proj_weight, dtype=np.float32)
    ipb = np.asarray(L0_attn_in_proj_bias, dtype=np.float32)

    nc = build_program(C)
    in_maps = []
    shared = {}
    for c in range(8):
        b, half = c // 2, c % 2
        if (b, half) not in shared:
            shared[(b, half)] = make_core_inputs(C, hs[b], u_all[b], ipw, ipb, ffn, half=half)
        in_maps.append(shared[(b, half)])

    res = run_bass_kernel_spmd(nc, in_maps, core_ids=list(range(8)))

    OWN = T // 2
    out = np.empty((B, T, D), dtype=np.float32)
    for c in range(8):
        b, half = c // 2, c % 2
        out[b, half * OWN:(half + 1) * OWN, :] = np.asarray(res.results[c]["y"])
    return out
